# version 22
# baseline (speedup 1.0000x reference)
import numpy as np
from contextlib import ExitStack

import concourse.bass as bass
import concourse.mybir as mybir
from concourse.bass_utils import run_bass_kernel_spmd

F32 = mybir.dt.float32
BF16 = mybir.dt.bfloat16
OP = mybir.AluOpType
AF = mybir.ActivationFunctionType
AXX = mybir.AxisListType.X

MAGIC = 12582912.0
DM = 1024
G = 8
NT = 16
L = 4096
CW = 512
NCH = L // CW
HALF = L // 2
F = 4096
FT = F // 128
EPS_LN = 1e-5
EPS_Q = 1e-8
INV_D = 1.0 / 1024.0
PAIRS = [[0, 1], [2, 3], [4, 5], [6, 7]]
ALL8 = [list(range(8))]


class Prog:

    STREAMS = {
        "dV": "sync", "dX": "sync", "dW": "sync", "dRow": "sync",
        "dS1": "sync", "dS2": "sync", "dOP1": "sync", "dOP2": "sync",
        "dH1": "sync", "dH2": "sync", "dH3": "sync", "dH4": "sync",
        "dOut1": "sync", "dOut2": "sync", "dMx": "sync",
        "aW": "scalar", "aWs1": "scalar", "aWs2": "scalar",
        "aB1": "scalar", "aB2": "scalar",
        "aC1": "scalar", "aC2": "scalar", "aD0": "scalar",
        "aXC": "scalar", "aSZ": "scalar", "aOP1": "scalar", "aOP2": "scalar",
        "aH1": "scalar", "aH2": "scalar", "aH3": "scalar", "aH4": "scalar",
        "aW1": "scalar", "aW2": "scalar",
        "gB": "gpsimd",
    }

    def __init__(self, nc, es):
        self.nc = nc
        self.ops = {e: [] for e in ("sync", "scalar", "vector", "gpsimd", "tensor")}
        self.sems = {}
        self.cnt = {}
        self.pos = {e: 0 for e in self.ops}
        self.last_drain = {e: -1 for e in self.ops}
        self.waited = {e: {} for e in self.ops}
        names = (["sT", "sV", "sC", "sG", "sCC", "bS", "bA", "bV", "bG", "bT"]
                 + list(self.STREAMS))
        for s in names:
            self.sems[s] = es.enter_context(nc.semaphore(s))
            self.cnt[s] = 0

    def op(self, eng, fn, waits=(), inc=None, inc_n=1):
        my = self.ops[eng]
        nw = []
        need_drain = False
        for t in waits:
            if t is None:
                continue
            s, v, peng, ppos = t
            if peng == eng and s not in self.STREAMS:
                if ppos > self.last_drain[eng]:
                    need_drain = True
            else:
                if self.waited[eng].get(s, -1) < v:
                    nw.append((s, v))
                    self.waited[eng][s] = v
        if need_drain:
            my.append(("drain", None, None, None))
            self.last_drain[eng] = self.pos[eng]
            self.pos[eng] += 1
        tok = None
        if inc is not None:
            self.cnt[inc] += inc_n
            tok = (inc, self.cnt[inc], eng, self.pos[eng])
        my.append(("op", fn, nw, (inc, inc_n) if inc else None))
        self.pos[eng] += 1
        return tok

    def dma(self, stream, dst, src, waits=()):
        q = self.STREAMS[stream]
        waits = list(waits)
        if self.cnt[stream] > 0:
            waits.append((stream, self.cnt[stream], "_dma_", -1))
        return self.op(q, lambda e, d=dst, s=src: e.dma_start(out=d, in_=s),
                       waits=waits, inc=stream, inc_n=16)

    def barrier(self):
        bs = {"sync": "bS", "scalar": "bA", "vector": "bV",
              "gpsimd": "bG", "tensor": "bT"}
        dw = [(s, self.cnt[s], "_dma_", -1) for s in self.STREAMS
              if self.cnt[s] > 0]
        toks = []
        for e in self.ops:
            self.ops[e].append(("drain", None, None, None))
            self.last_drain[e] = self.pos[e]
            self.pos[e] += 1
            toks.append(self.op(e, lambda eng: eng.nop(),
                                waits=[t for t in dw], inc=bs[e]))
        for e in self.ops:
            self.op(e, None, waits=[t for t in toks if t[2] != e])
            self.last_drain[e] = self.pos[e] - 1
        return toks

    def replay(self, block):
        def mk(eng_name):
            def body(eng):
                for kind, fn, waits, inc in self.ops[eng_name]:
                    if kind == "drain":
                        eng.drain()
                        continue
                    for s, v in waits:
                        eng.wait_ge(self.sems[s], v)
                    if fn is None:
                        continue
                    instr = fn(eng)
                    if inc is not None and instr is not None:
                        instr.then_inc(self.sems[inc[0]], inc[1])
            return body
        block.sync(mk("sync"))
        block.scalar(mk("scalar"))
        block.vector(mk("vector"))
        block.gpsimd(mk("gpsimd"))
        block.tensor(mk("tensor"))


def build_nc():
    nc = bass.Bass("TRN2", target_bir_lowering=False, debug=False)

    def dr_in(name, shape, dt=F32):
        return nc.dram_tensor(name, shape, dt, kind="ExternalInput").ap()

    xT = dr_in("xT", [DM, L])
    xmine = dr_in("xmine", [DM, HALF])
    winm = dr_in("winm", [DM, 2048])
    xpw = dr_in("xpw", [1024, 33])
    woutm = dr_in("woutm", [1024, DM])
    w1n = dr_in("w1n", [F, DM])
    w2n = dr_in("w2n", [DM, F])
    alog2d = dr_in("alog2d", [128, 128])
    ident_in = dr_in("ident", [128, 128], BF16)
    vec_shapes = {"cw0": G, "cw1": G, "cw2": G, "cw3": G, "cb": G, "dv": G,
                  "dtw": G, "dtb": G, "nw": G, "nb": G, "b1v": FT, "b2v": G}
    vec = {nm: dr_in(nm, [128, g]) for nm, g in vec_shapes.items()}
    out_h = nc.dram_tensor("out", [DM, HALF], F32, kind="ExternalOutput").ap()

    def dr(name, shape, dt=F32):
        return nc.dram_tensor(name, shape, dt).ap()

    xc_h = dr("xc_h", [1024, L])
    sz_h = dr("sz_h", [1024, L])
    dbc_bnc = dr("dbc_bnc", [33, L])
    dbc_red = dr("dbc_red", [33, L])
    murow_h = dr("murow_h", [1, L])
    s2row_h = dr("s2row_h", [1, 4096])
    dt_hbm = dr("dt_hbm", [1024, L])
    rrow_h = dr("rrow_h", [1, L])
    w1q_h = dr("w1q_h", [DM, F], BF16)
    w2q_h = dr("w2q_h", [F, DM], BF16)
    h_h = dr("h_h", [F, HALF])
    opin = [dr(f"opin{i}", [2, DM, CW]) for i in range(4)]
    opout = [dr(f"opout{i}", [DM, CW]) for i in range(4)]
    mu2row_h = dr("mu2row_h", [1, HALF])
    rst2row_h = dr("rst2row_h", [1, HALF])
    vbnc = dr("vbnc", [128, 1])
    mx_in = dr("mx_in", [1, 1])
    mx_out = dr("mx_out", [1, 1])
    mx2_in = dr("mx2_in", [1, 1])
    mx2_out = dr("mx2_out", [1, 1])

    es = ExitStack()
    with es:
        P = Prog(nc, es)

        def SB(name, shape, dt=F32, stack=None):
            return (stack or es).enter_context(nc.sbuf_tensor(name, shape, dt))

        def PS(name, shape, dt=F32, stack=None):
            return (stack or es).enter_context(nc.psum_tensor(name, shape, dt))

        cst_sb = SB("cst_sb", [128, 4])
        for i, cv in enumerate([EPS_LN, 1.0]):
            nc.const_aps.register(F32, cv, cst_sb[:, i:i + 1])
        ones_sb = SB("ones_sb", [128, 1])
        ident = SB("ident_sb", [128, 128], BF16)
        asb = SB("asb", [128, 128])
        vsb = {nm: SB(nm + "_sb", [128, vec_shapes[nm]]) for nm in vec}
        carry = SB("carry", [128, 128])
        sw1 = SB("sw1", [128, FT])
        sw2 = SB("sw2", [128, G])
        scal = SB("scal", [128, 8])
        srow = SB("srow", [1, 16])


        P.op("vector", lambda e: e.memset(cst_sb[:, 0:1], EPS_LN), inc="sV")
        P.op("vector", lambda e: e.memset(cst_sb[:, 1:2], 1.0), inc="sV")
        t_ld = {nm: P.dma("dV", vsb[nm][:, :], vec[nm]) for nm in vec}
        t_alog = P.dma("dV", asb[:, :], alog2d)
        t_id = P.dma("dV", ident[:, :], ident_in)
        P.op("vector", lambda e: e.memset(ones_sb[:, :], 1.0), inc="sV")
        t_carry0 = P.op("vector", lambda e: e.memset(carry[:, :], 0.0), inc="sV")
        t_a1 = P.op("scalar", lambda e: e.activation(asb[:, :], asb[:, :], AF.Exp),
                    waits=[t_alog], inc="sC")
        t_a2 = P.op("vector",
                    lambda e: e.tensor_scalar(asb[:, :], asb[:, :], -1.0, None, OP.mult),
                    waits=[t_a1], inc="sV")

        q_st = {"pc": [None, None], "pst": [None, None], "tr_i": 0}
        st1 = {"q2": None, "qread": None}
        st2 = {"q2": None, "qread": None}

        with ExitStack() as p1:
            winm_sb = SB("winm_sb", [128, G, 2048], stack=p1)
            xpw_sb = SB("xpw_sb", [128, G, 33], stack=p1)
            xch = SB("xch", [128, 2, G, CW], stack=p1)
            xsq = SB("xsq", [128, 2, CW], stack=p1)
            pad = SB("pad", [128, G, 515], stack=p1)
            convo = SB("convo", [128, CW], stack=p1)
            xcs = SB("xcs", [128, 8, CW], stack=p1)
            szc = SB("szc", [128, 2, CW], stack=p1)
            evt = SB("evt", [128, 2, CW], stack=p1)
            rbp = SB("rbp", [128, 2, CW], stack=p1)
            rmbp = SB("rmbp", [128, 2, CW], stack=p1)
            dbcp = SB("dbcp", [33, 2, CW], stack=p1)
            s1c = SB("s1c", [128, 32], stack=p1)
            cbp2 = SB("cbp2", [128, 8], stack=p1)
            st_r = SB("st_r", [1, 7 * CW], stack=p1)
            psA = PS("psA", [128, CW], stack=p1)
            psB = PS("psB", [128, CW], stack=p1)
            psDBC = PS("psDBC", [33, CW], stack=p1)
            psSX = PS("psSX", [1, CW], stack=p1)
            psSX2 = PS("psSX2", [1, CW], stack=p1)
            psS12 = PS("psS12", [1, CW], stack=p1)

            wnat = SB("wnat", [128, DM], stack=p1)
            wnat2 = SB("wnat2", [128, F], stack=p1)
            wqn = SB("wqn", [128, DM], BF16, stack=p1)
            wqn2 = SB("wqn2", [128, F], BF16, stack=p1)
            tstage = SB("tstage", [128, 2, 8 * 128], BF16, stack=p1)
            rcol = SB("rcol", [128, 2], stack=p1)
            psTR = [PS("psTR0", [128, 1024], BF16, stack=p1),
                    PS("psTR1", [128, 1024], BF16, stack=p1)]

            def quant_tile(src_ap, dst_nat, dst_q, scol, sl, st):
                t_l = P.dma("aW", dst_nat, src_ap, waits=[st["q2"]])
                t_red = P.op("vector",
                             lambda e: e.tensor_reduce(scol, dst_nat, AXX, OP.max,
                                                       apply_absolute_value=True),
                             waits=[t_l], inc="sV")
                t_m = P.op("vector",
                           lambda e: e.tensor_scalar(scol, scol, EPS_Q, 1.0 / 127.0,
                                                     OP.max, OP.mult),
                           waits=[t_red], inc="sV")
                t_rc = P.op("vector",
                            lambda e: e.reciprocal(rcol[:, sl:sl + 1], scol),
                            waits=[t_m], inc="sV")
                t_q1 = P.op("vector",
                            lambda e: e.tensor_scalar(dst_nat, dst_nat,
                                                      rcol[:, sl:sl + 1], MAGIC,
                                                      OP.mult, OP.add),
                            waits=[t_rc], inc="sV")
                t_q2 = P.op("vector",
                            lambda e: e.tensor_scalar(dst_q, dst_nat, MAGIC,
                                                      None, OP.subtract),
                            waits=[t_q1, st["qread"]], inc="sV")
                st["q2"] = t_q2
                return t_q2

            def w1_A(ft):
                quant_tile(w1n[ft * 128:(ft + 1) * 128, :], wnat[:, :],
                           wqn[:, :], sw1[:, ft:ft + 1], ft % 2, st1)

            def w1_B(ft):
                t_q2 = st1["q2"]
                sl = ft % 2
                t_pcl = None
                t_tr = None
                for hg in range(2):
                    s4 = q_st["tr_i"] % 2
                    q_st["tr_i"] += 1
                    for q in range(4):
                        g = hg * 4 + q
                        t_tr = P.op("tensor",
                                    lambda e, g=g, s4=s4, q=q: e.transpose(
                                        psTR[s4][:, q * 256:q * 256 + 128],
                                        wqn[:, g * 128:(g + 1) * 128],
                                        ident[:, :]),
                                    waits=[t_q2, t_id,
                                           q_st["pc"][s4] if q == 0 else None],
                                    inc="sT")
                    q_st["pc"][s4] = P.op("scalar",
                                          lambda e, hg=hg, sl=sl, s4=s4: e.copy(
                                              tstage[:, sl, hg * 512:(hg + 1) * 512]
                                              .rearrange("p (q c) -> p q c", q=4),
                                              psTR[s4][:, :]
                                              .rearrange("p (q c) -> p q c", q=4)
                                              [:, :, 0:128]),
                                          waits=[t_tr, q_st["pst"][sl]], inc="sC")
                    t_pcl = q_st["pc"][s4]
                st1["qread"] = t_tr
                q_st["pst"][sl] = P.dma("aWs1" if sl == 0 else "aWs2",
                                        w1q_h.rearrange("(g p) f -> p g f", p=128)
                                        [:, :, ft * 128:(ft + 1) * 128],
                                        tstage[:, sl, :]
                                        .rearrange("p (g c) -> p g c", g=G),
                                        waits=[t_pcl])

            def w2_A(g):
                quant_tile(w2n[g * 128:(g + 1) * 128, :], wnat2[:, :],
                           wqn2[:, :], sw2[:, g:g + 1], g % 2, st2)

            def w2_B(g):
                t_q2 = st2["q2"]
                sl = g % 2
                t_tr = None
                for fh in range(8):
                    s4 = q_st["tr_i"] % 2
                    q_st["tr_i"] += 1
                    for q in range(4):
                        ft = fh * 4 + q
                        t_tr = P.op("tensor",
                                    lambda e, ft=ft, s4=s4, q=q: e.transpose(
                                        psTR[s4][:, q * 256:q * 256 + 128],
                                        wqn2[:, ft * 128:(ft + 1) * 128],
                                        ident[:, :]),
                                    waits=[t_q2,
                                           q_st["pc"][s4] if q == 0 else None],
                                    inc="sT")
                    hh = fh % 2
                    q_st["pc"][s4] = P.op("scalar",
                                          lambda e, hh=hh, sl=sl, s4=s4: e.copy(
                                              tstage[:, sl, hh * 512:(hh + 1) * 512]
                                              .rearrange("p (q c) -> p q c", q=4),
                                              psTR[s4][:, :]
                                              .rearrange("p (q c) -> p q c", q=4)
                                              [:, :, 0:128]),
                                          waits=[t_tr, q_st["pst"][sl]], inc="sC")
                    if hh == 1:
                        q_st["pst"][sl] = P.dma("aWs1" if fh % 4 == 1 else "aWs2",
                                                w2q_h.rearrange("(fg p) d -> p fg d",
                                                                p=128)
                                                [:, fh * 4 - 4:fh * 4 + 4,
                                                 g * 128:(g + 1) * 128],
                                                tstage[:, sl, :]
                                                .rearrange("p (s c) -> p s c", s=8),
                                                waits=[q_st["pc"][s4]])
                st2["qread"] = t_tr

            p0_units = ([(lambda ft=ft: w1_A(ft), lambda ft=ft: w1_B(ft))
                         for ft in range(FT)] +
                        [(lambda g=g: w2_A(g), lambda g=g: w2_B(g))
                         for g in range(G)])
            p0_i = [0]
            p0_a = [0]

            def pump_A():
                if p0_a[0] < len(p0_units) and p0_a[0] == p0_i[0]:
                    p0_units[p0_a[0]][0]()
                    p0_a[0] += 1

            def pump_B():
                if p0_i[0] < p0_a[0]:
                    p0_units[p0_i[0]][1]()
                    p0_i[0] += 1

            def row(i):
                return st_r[:, i * CW:(i + 1) * CW]

            t_winm = P.dma("dW", winm_sb[:, :, :],
                           winm.rearrange("(g p) c -> p g c", p=128))
            t_xpw = P.dma("dW", xpw_sb[:, :, :],
                          xpw.rearrange("(g p) c -> p g c", p=128))
            t_padz = P.op("vector", lambda e: e.memset(pad[:, :, 0:3], 0.0), inc="sV")

            t_s2m = None
            t_s1m = None
            t_wsc = [None] * G
            for cc in range(4):
                ccs = slice(cc * CW, (cc + 1) * CW)
                for g in range(G):
                    t_s2m = P.op("tensor",
                                 lambda e, g=g, ccs=ccs: e.matmul(
                                     psS12[:, :], vsb["nb"][:, g:g + 1],
                                     winm_sb[:, g, ccs], start=(g == 0),
                                     stop=(g == G - 1)),
                                 waits=[t_winm, t_ld["nb"],
                                        t_s2ev if cc > 0 else None], inc="sT")
                t_s2ev = P.op("vector",
                              lambda e, ccs=ccs: e.tensor_copy(st_r[:, ccs],
                                                               psS12[:, :]),
                              waits=[t_s2m], inc="sV")
            for g in range(G):
                t_wsc[g] = P.op("vector",
                                lambda e, g=g: e.tensor_scalar(
                                    winm_sb[:, g, :], winm_sb[:, g, :],
                                    vsb["nw"][:, g:g + 1], None, OP.mult),
                                waits=[t_winm, t_ld["nw"], t_s2ev], inc="sV")
            t_s2s = P.dma("dMx", s2row_h[:, 2048:4096], st_r[:, 0:4 * CW],
                          waits=[t_s2ev])
            t_s1ev = None
            for cc in range(4):
                ccs = slice(cc * CW, (cc + 1) * CW)
                for g in range(G):
                    t_s1m = P.op("tensor",
                                 lambda e, g=g, ccs=ccs: e.matmul(
                                     psS12[:, :], ones_sb[:, :],
                                     winm_sb[:, g, ccs], start=(g == 0),
                                     stop=(g == G - 1)),
                                 waits=[t_wsc[g], t_s2ev,
                                        t_s1ev if cc > 0 else None], inc="sT")
                t_s1ev = P.op("vector",
                              lambda e, cc=cc: e.tensor_copy(
                                  st_r[:, cc * CW:(cc + 1) * CW], psS12[:, :]),
                              waits=[t_s1m, t_s2s], inc="sV")
            t_s1s = P.dma("dMx", s2row_h[:, 0:2048], st_r[:, 0:4 * CW],
                          waits=[t_s1ev, t_s2s])
            t_s1l = P.op("sync",
                         lambda e: e.dma_start(
                             out=s1c[:, :],
                             in_=s2row_h.rearrange("one (c p) -> p (one c)", p=128),
                             allow_slow_non_contiguous=True),
                         waits=[("dMx", P.cnt["dMx"], "_dma_", -1)],
                         inc="dMx", inc_n=16)
            t_s1n = P.op("vector",
                         lambda e: e.tensor_scalar(s1c[:, 0:16], s1c[:, 0:16], -1.0,
                                                   None, OP.mult),
                         waits=[t_s1l], inc="sV")
            t_cb1 = P.op("vector",
                         lambda e: e.tensor_tensor(cbp2[:, :], vsb["cw0"][:, :],
                                                   vsb["cw1"][:, :], OP.add),
                         waits=[t_ld["cw0"], t_ld["cw1"]], inc="sV")
            t_cb2 = P.op("vector",
                         lambda e: e.tensor_tensor(cbp2[:, :], cbp2[:, :],
                                                   vsb["cw2"][:, :], OP.add),
                         waits=[t_cb1, t_ld["cw2"]], inc="sV")
            t_cb3 = P.op("vector",
                         lambda e: e.tensor_tensor(cbp2[:, :], cbp2[:, :],
                                                   vsb["cw3"][:, :], OP.add),
                         waits=[t_cb2, t_ld["cw3"]], inc="sV")
            t_cb4 = P.op("vector",
                         lambda e: e.tensor_tensor(cbp2[:, :], cbp2[:, :],
                                                   s1c[:, 16:24], OP.mult),
                         waits=[t_cb3, t_s1n], inc="sV")
            t_cb5 = P.op("vector",
                         lambda e: e.tensor_tensor(cbp2[:, :], cbp2[:, :],
                                                   vsb["cb"][:, :], OP.add),
                         waits=[t_cb4, t_ld["cb"]], inc="sV")
            t_padf = P.op("vector",
                          lambda e: e.tensor_scalar(
                              pad[:, :, 0:3],
                              s1c[:, 16:24].unsqueeze(2).broadcast_to([128, 8, 3]),
                              -1.0, None, OP.mult),
                          waits=[t_padz, t_s1l], inc="sV")

            t_xcs_free = [[] for _ in range(8)]
            t_szc_free = [None, None]
            t_sils = [None] * 8
            t_stxs = [None] * 8
            t_psA_free = None
            t_psB_free = None
            t_dbc_evac = None
            t_dbc_st = [None, None]
            t_sq_rd = [None, None]
            t_stats_done = None
            t_row_free = []
            t_xch_free = [[], []]
            t_plane_free = [[], []]
            t_tail = [None] * G
            t_convo_free = None
            t_dbc_mm = None
            t_evt_free = [None, None]
            t_xl = [None] * NCH
            t_smm = [None] * NCH
            t_bcast = [None] * NCH

            def p1_load(c):
                xsl = c % 2
                t_xl[c] = P.dma("dX", xch[:, xsl, :, :],
                                xT.rearrange("(g p) t -> p g t", p=128)
                                [:, :, c * CW:(c + 1) * CW],
                                waits=t_xch_free[xsl])

            def p1_stats(c):
                nonlocal t_stats_done, t_row_free
                xsl = c % 2
                cs = slice(c * CW, (c + 1) * CW)
                t_x = t_xl[c]
                t_sq = [None, None]
                t_mmx = None
                t_mmx2 = None
                for g in range(G):
                    sl = g % 2
                    t_sq[sl] = P.op("vector",
                                    lambda e, g=g, sl=sl, xsl=xsl: e.tensor_tensor(
                                        xsq[:, sl, :], xch[:, xsl, g, :],
                                        xch[:, xsl, g, :], OP.mult),
                                    waits=[t_x, t_sq_rd[sl]], inc="sV")
                    t_mmx = P.op("tensor",
                                 lambda e, g=g, xsl=xsl, ps=psSX: e.matmul(
                                     ps[:, :], ones_sb[:, :], xch[:, xsl, g, :],
                                     start=(g == 0), stop=(g == G - 1)),
                                 waits=[t_x, t_stats_done if g == 0 else None],
                                 inc="sT")
                    t_mmx2 = P.op("tensor",
                                  lambda e, g=g, sl=sl, ps=psSX2: e.matmul(
                                      ps[:, :], ones_sb[:, :], xsq[:, sl, :],
                                      start=(g == 0), stop=(g == G - 1)),
                                  waits=[t_sq[sl], t_stats_done if g == 0 else None],
                                  inc="sT")
                    t_sq_rd[sl] = t_mmx2
                t_smm[c] = t_mmx
                t_mu = P.op("vector",
                            lambda e, ps=psSX: e.tensor_scalar(row(0), ps[:, :],
                                                               INV_D, None, OP.mult),
                            waits=[t_mmx] + t_row_free, inc="sV")
                t_e2 = P.op("vector",
                            lambda e, ps=psSX2: e.tensor_scalar(row(1), ps[:, :],
                                                                INV_D, None, OP.mult),
                            waits=[t_mmx2], inc="sV")
                t_m2 = P.op("vector", lambda e: e.tensor_tensor(row(2), row(0), row(0),
                                                                OP.mult),
                            waits=[t_mu], inc="sV")
                t_v1 = P.op("vector", lambda e: e.tensor_tensor(row(3), row(1), row(2),
                                                                OP.subtract),
                            waits=[t_e2, t_m2], inc="sV")
                t_ln = P.op("scalar", lambda e: e.activation(row(2), row(3), AF.Ln,
                                                             bias=EPS_LN),
                            waits=[t_v1], inc="sC")
                t_r1 = P.op("scalar", lambda e: e.activation(row(4), row(2), AF.Exp,
                                                             scale=-0.5),
                            waits=[t_ln], inc="sC")
                t_v2a = P.op("vector", lambda e: e.tensor_tensor(row(2), row(4), row(4),
                                                                 OP.mult),
                             waits=[t_r1], inc="sV")
                t_v2 = P.op("vector", lambda e: e.tensor_tensor(row(2), row(3), row(2),
                                                                OP.mult),
                            waits=[t_v2a], inc="sV")
                t_ln2 = P.op("scalar", lambda e: e.activation(row(2), row(2), AF.Ln,
                                                              bias=EPS_LN),
                             waits=[t_v2], inc="sC")
                t_r2 = P.op("scalar", lambda e: e.activation(row(5), row(2), AF.Exp,
                                                             scale=-0.5),
                             waits=[t_ln2], inc="sC")
                t_R = P.op("vector", lambda e: e.tensor_tensor(row(6), row(4), row(5),
                                                               OP.mult),
                           waits=[t_r2], inc="sV")
                t_RM = P.op("vector", lambda e: e.tensor_tensor(row(2), row(6), row(0),
                                                                OP.mult),
                            waits=[t_R], inc="sV")
                t_stats_done = t_e2
                t_sR = P.dma("dRow", rrow_h[:, cs], row(6), waits=[t_R])
                t_sRM = P.dma("dRow", murow_h[:, cs], row(2), waits=[t_RM])
                t_bR = P.dma("dRow", rbp[:, xsl, :],
                             rrow_h[:, cs].squeeze(0).partition_broadcast(128),
                             waits=[t_sR] + t_plane_free[xsl])
                t_bRM = P.dma("dRow", rmbp[:, xsl, :],
                              murow_h[:, cs].squeeze(0).partition_broadcast(128),
                              waits=[t_sRM])
                t_row_free = [t_sR, t_sRM]
                t_bcast[c] = (t_bR, t_bRM)

            p1_load(0)
            p1_stats(0)
            p1_load(1)
            for c in range(NCH):
                xsl = c % 2
                cs = slice(c * CW, (c + 1) * CW)
                if c + 1 < NCH:
                    p1_stats(c + 1)
                t_x = t_xl[c]
                t_bR, t_bRM = t_bcast[c]
                t_mm_last = None
                for ct in range(16):
                    it = c * 16 + ct
                    if it % 3 == 0:
                        pump_A()
                    elif it % 3 == 2:
                        pump_B()
                    ps = psA if ct % 2 == 0 else psB
                    ps_free = t_psA_free if ct % 2 == 0 else t_psB_free
                    t_mm = None
                    for g in range(G):
                        t_mm = P.op("tensor",
                                    lambda e, g=g, ct=ct, ps=ps, xsl=xsl: e.matmul(
                                        ps[:, :],
                                        winm_sb[:, g, ct * 128:(ct + 1) * 128],
                                        xch[:, xsl, g, :], start=(g == 0),
                                        stop=(g == G - 1)),
                                    waits=[t_x, t_wsc[g],
                                           ps_free if g == 0 else None],
                                    inc="sT")
                    t_mm_last = t_mm
                    esl = ct % 2
                    t_e1 = P.op("vector",
                                lambda e, esl=esl, ps=ps, xsl=xsl: e.tensor_tensor(
                                    evt[:, esl, :], ps[:, :], rbp[:, xsl, :],
                                    OP.mult),
                                waits=[t_mm, t_bR, t_evt_free[esl]], inc="sV")
                    if ct < 8:
                        dst = pad[:, ct, 3:515]
                    else:
                        dst = None
                    if ct < 8:
                        t_e2v = P.op("vector",
                                     lambda e, ct=ct, esl=esl, xsl=xsl:
                                     e.scalar_tensor_tensor(
                                         pad[:, ct, 3:515], rmbp[:, xsl, :],
                                         s1c[:, ct:ct + 1], evt[:, esl, :],
                                         OP.mult, OP.add),
                                     waits=[t_e1, t_bRM, t_s1n, t_padf,
                                            t_tail[ct]], inc="sV")
                        t_evt_free[esl] = t_e2v
                        t_c = P.op("vector",
                                   lambda e, ct=ct: e.tensor_scalar(
                                       convo[:, :], pad[:, ct, 0:CW],
                                       vsb["cw0"][:, ct:ct + 1], cbp2[:, ct:ct + 1],
                                       OP.mult, OP.add),
                                   waits=[t_e2v, t_ld["cw0"], t_cb5,
                                          t_convo_free], inc="sV")
                        for k in (1, 2, 3):
                            t_c = P.op("vector",
                                       lambda e, ct=ct, k=k: e.scalar_tensor_tensor(
                                           convo[:, :], pad[:, ct, k:k + CW],
                                           vsb["cw%d" % k][:, ct:ct + 1], convo[:, :],
                                           OP.mult, OP.add),
                                       waits=[t_c, t_ld["cw%d" % k]], inc="sV")
                        t_tail[ct] = P.op("vector",
                                          lambda e, ct=ct: e.tensor_copy(
                                              pad[:, ct, 0:3], pad[:, ct, 512:515]),
                                          waits=[t_c], inc="sV")
                        t_sil = P.op("scalar",
                                     lambda e, ct=ct: e.activation(xcs[:, ct, :],
                                                                   convo[:, :], AF.Silu),
                                     waits=[t_c] + t_xcs_free[ct], inc="sC")
                        t_convo_free = t_sil
                        t_stx = P.dma("dS1",
                                      xc_h.rearrange("(g p) t -> p g t", p=128)[:, ct, cs],
                                      xcs[:, ct, :], waits=[t_sil])
                        t_sils[ct] = t_sil
                        t_stxs[ct] = t_stx
                        if ct % 2 == 0:
                            t_psA_free = t_e1
                        else:
                            t_psB_free = t_e1
                    else:
                        g2 = ct - 8
                        sl2 = g2 % 2
                        t_e2v = P.op("vector",
                                     lambda e, ct=ct, esl=esl, xsl=xsl:
                                     e.scalar_tensor_tensor(
                                         evt[:, esl, :], rmbp[:, xsl, :],
                                         s1c[:, ct:ct + 1], evt[:, esl, :],
                                         OP.mult, OP.add),
                                     waits=[t_e1, t_bRM, t_s1n], inc="sV")
                        t_sz = P.op("scalar",
                                    lambda e, sl2=sl2, esl=esl, ct=ct: e.activation(
                                        szc[:, sl2, :], evt[:, esl, :], AF.Silu,
                                        bias=s1c[:, 16 + ct:17 + ct]),
                                    waits=[t_e2v, t_s1l, t_szc_free[sl2]], inc="sC")
                        t_evt_free[esl] = t_sz
                        t_szd = P.dma("dS2",
                                      sz_h.rearrange("(g p) t -> p g t", p=128)[:, g2, cs],
                                      szc[:, sl2, :], waits=[t_sz])
                        t_szc_free[sl2] = t_szd
                        if ct % 2 == 0:
                            t_psA_free = t_e1
                        else:
                            t_psB_free = t_e1
                t_plane_free[xsl] = [t_e2v]
                t_xch_free[xsl] = [t_smm[c], t_mm_last]
                if c + 2 < NCH:
                    p1_load(c + 2)
                for ct in range(8):
                    t_xp = P.op("tensor",
                                lambda e, ct=ct: e.matmul(
                                    psDBC[:, :], xpw_sb[:, ct, :], xcs[:, ct, :],
                                    start=(ct == 0), stop=(ct == 7)),
                                waits=[t_sils[ct], t_xpw,
                                       t_dbc_evac if ct == 0 else None],
                                inc="sT")
                    t_xcs_free[ct] = [t_stxs[ct], t_xp]
                dsl = c % 2
                t_dbc_evac = P.op("scalar",
                                  lambda e, dsl=dsl: e.copy(dbcp[:, dsl, :],
                                                            psDBC[:, :]),
                                  waits=[t_xp, t_dbc_st[dsl]], inc="sC")
                t_dbc_st[dsl] = P.dma("gB", dbc_bnc[:, cs], dbcp[:, dsl, :],
                                      waits=[t_dbc_evac])
            while p0_i[0] < len(p0_units):
                pump_A()
                pump_B()
            t_dbcs = t_dbc_st[(NCH - 1) % 2]
            t_ar = P.op("gpsimd",
                        lambda e: e.collective_compute(
                            "AllReduce", OP.add, replica_groups=PAIRS,
                            ins=[dbc_bnc.opt()], outs=[dbc_red.opt()]),
                        waits=[t_dbcs], inc="sCC")
            P.barrier()

        with ExitStack() as p2:
            wotile = SB("wotile", [128, 2, G, 128], stack=p2)
            xcc = SB("xcc", [128, 2, G, CW], stack=p2)
            szz = SB("szz", [128, G, CW], stack=p2)
            dt_sb = SB("dt_sb", [128, G, CW], stack=p2)
            dtx = SB("dtx", [128, G, CW], stack=p2)
            d0p = SB("d0p", [128, 2, CW], stack=p2)
            bbuf = SB("bbuf", [128, 2, CW], stack=p2)
            cbuf = SB("cbuf", [128, 2, CW], stack=p2)
            ubuf = SB("ubuf", [128, 2, G, CW], stack=p2)
            hbuf = SB("hbuf", [128, G, CW], stack=p2)
            tbuf = SB("tbuf", [128, G, CW], stack=p2)
            ybuf = SB("ybuf", [128, G, CW], stack=p2)
            y2c = SB("y2c", [128, G, CW], stack=p2)
            wos = SB("wos", [128, 2, CW], stack=p2)
            psA = PS("psA2", [128, CW], stack=p2)
            psB = PS("psB2", [128, CW], stack=p2)
            psab = [PS("psab0", [128, CW], stack=p2), PS("psab1", [128, CW], stack=p2)]
            pscb = [PS("pscb0", [128, CW], stack=p2), PS("pscb1", [128, CW], stack=p2)]
            psbb = [PS("psbb0", [128, CW], stack=p2), PS("psbb1", [128, CW], stack=p2)]

            def bc(sb, sl, gn):
                return sb[:, sl, :].unsqueeze(1).broadcast_to([128, gn, CW])

            def bcp(ps):
                return ps[:, :].unsqueeze(1).broadcast_to([128, G, CW])

            GH = G // 2
            t_b_free = [None, None]
            t_c_free = [None, None]
            t_carryn = [t_carry0] * NT
            t_d0_free = [None, None]
            t_dt_free = [None]
            t_dtx_free = None
            t_ybuf_free = None
            t_scanned = None
            t_scan_hist = [None, None]
            t_abr = [None, None]
            t_abw = [None, None]
            t_cbr = [None, None]
            t_bbr = [None, None]
            t_xcc_free = [[], []]
            t_szz_free = None
            t_tb_free = None
            t_y2c_free = None
            t_wo_mm = None
            t_wos_free = [None, None]
            wo_pend = []
            t_cm = None
            t_yacc = None
            t_cm_pend = None
            t_wosf = None
            t_wo_free = [None, None]
            t_ps_prev = [None, None]
            t_dtld = {}
            t_xcl = [None] * (NCH + 1)
            t_d0l = [None] * (NCH + 1)

            def emit_wout(cw, t_y2w, m):
                nonlocal t_wosf, t_wo_mm
                hb = cw // 4
                cp = cw % 4
                wsl = m % 2
                t_wl = P.dma("dW", wotile[:, wsl, :, :],
                             woutm.rearrange("(g p) m -> p g m", p=128)
                             [:, :, m * 128:(m + 1) * 128],
                             waits=[t_wo_free[wsl]])
                ps = psA if m % 2 == 0 else psB
                t_mm = None
                for g in range(G):
                    t_mm = P.op("tensor",
                                lambda e, g=g, m=m, ps=ps, wsl=wsl: e.matmul(
                                    ps[:, :], wotile[:, wsl, g, :],
                                    y2c[:, g, :], start=(g == 0),
                                    stop=(g == G - 1)),
                                waits=[t_y2w, t_wl,
                                       t_ps_prev[wsl] if g == 0 else None],
                                inc="sT")
                t_wo_free[wsl] = t_mm
                t_wo_mm = t_mm
                t_ev = P.op("scalar",
                            lambda e, ps=ps, wsl=wsl: e.copy(wos[:, wsl, :],
                                                             ps[:, :]),
                            waits=[t_mm, t_wos_free[wsl]], inc="sC")
                t_ps_prev[wsl] = t_ev
                t_std = P.dma("dOP1" if m % 2 == 0 else "dOP2",
                              opin[cp][hb, m * 128:(m + 1) * 128, :],
                              wos[:, wsl, :], waits=[t_ev])
                t_wos_free[wsl] = t_std
                t_wosf = t_std

            def load_chunk(c):
                if c >= NCH:
                    return
                csl = c % 2
                cs2 = slice(c * CW, (c + 1) * CW)
                t_xcl[c] = P.dma("aXC", xcc[:, csl, :, :],
                                 xc_h.rearrange("(g p) t -> p g t", p=128)[:, :, cs2],
                                 waits=t_xcc_free[csl])
                t_d0l[c] = P.dma("aD0", d0p[:, csl, :],
                                 dbc_red[0:1, cs2].squeeze(0).partition_broadcast(128),
                                 waits=[t_ar, t_d0_free[csl]])

            def build_dt(c, g, extra=()):
                csl = c % 2
                t1 = P.op("scalar",
                          lambda e, g=g, csl=csl, c=c: e.activation(
                              dt_sb[:, g, :], d0p[:, csl, :], AF.Exp,
                              scale=vsb["dtw"][:, g:g + 1],
                              bias=vsb["dtb"][:, g:g + 1]),
                          waits=[t_d0l[c], t_ld["dtw"], t_ld["dtb"],
                                 t_dt_free[0]] + list(extra), inc="sC")
                t2 = P.op("vector",
                          lambda e, g=g, csl=csl: e.tensor_scalar(
                              dt_sb[:, g, :], dt_sb[:, g, :], 1.6e19, None,
                              OP.min),
                          waits=[t1], inc="sV")
                t4 = P.op("scalar",
                          lambda e, g=g, csl=csl: e.activation(
                              dt_sb[:, g, :], dt_sb[:, g, :], AF.Ln,
                              bias=1.0),
                          waits=[t2], inc="sC")
                t_dtld.setdefault(c, [None] * G)[g] = t4
                if g == 0:
                    t_d0_free[csl] = t1

            load_chunk(0)
            load_chunk(1)
            for g in range(G):
                build_dt(0, g)
            for c in range(NCH):
                csl = c % 2
                cs = slice(c * CW, (c + 1) * CW)
                t_sz = P.dma("aSZ", szz[:, :, :],
                             sz_h.rearrange("(g p) t -> p g t", p=128)[:, :, cs],
                             waits=[t_szz_free])
                t_dt = t_dtld[c]
                t_dtx = P.op("vector",
                             lambda e, csl=csl: e.tensor_tensor(
                                 dtx[:, :, :], dt_sb[:, :, :],
                                 xcc[:, csl, :, :], OP.mult),
                             waits=[t_dt[G - 1], t_xcl[c], t_dtx_free], inc="sV")
                for n in range(NT):
                    sl = n % 2
                    if wo_pend and n < G:
                        emit_wout(wo_pend[0][0], wo_pend[0][1], n)
                        if n == G - 1:
                            t_y2c_free = t_wo_mm
                            wo_pend.pop(0)
                    t_bb = P.dma("aB1" if sl == 0 else "aB2", bbuf[:, sl, :],
                                 dbc_red[1 + n:2 + n, cs].squeeze(0)
                                 .partition_broadcast(128),
                                 waits=[t_b_free[sl]])
                    t_cb = P.dma("aC1" if sl == 0 else "aC2", cbuf[:, sl, :],
                                 dbc_red[17 + n:18 + n, cs].squeeze(0)
                                 .partition_broadcast(128),
                                 waits=[t_c_free[sl]])
                    t_cbp = P.op("scalar",
                                 lambda e, sl=sl: e.copy(pscb[sl][:, :],
                                                         cbuf[:, sl, :]),
                                 waits=[t_cb, t_cbr[sl]], inc="sC")
                    t_bbp = P.op("scalar",
                                 lambda e, sl=sl: e.copy(psbb[sl][:, :],
                                                         bbuf[:, sl, :]),
                                 waits=[t_bb, t_bbr[sl]], inc="sC")
                    t_uh = [None, None]
                    t_uh[0] = P.op("gpsimd",
                                   lambda e, sl=sl: e.tensor_tensor(
                                       ubuf[:, sl, 0:GH, :],
                                       dtx[:, 0:GH, :],
                                       bc(bbuf, sl, GH), OP.mult),
                                   waits=[t_dtx, t_bb, t_scan_hist[sl]],
                                   inc="sG")
                    t_uh[1] = P.op("vector",
                                   lambda e, sl=sl: e.tensor_tensor(
                                       ubuf[:, sl, GH:G, :],
                                       dtx[:, GH:G, :],
                                       psbb[sl][:, :].unsqueeze(1)
                                       .broadcast_to([128, GH, CW]), OP.mult),
                                   waits=[t_dtx, t_bbp, t_scan_hist[sl]],
                                   inc="sV")
                    t_bbr[sl] = t_uh[1]
                    if t_cm_pend is not None:
                        t_yacc = P.op("gpsimd",
                                      lambda e: e.tensor_tensor(
                                          ybuf[:, :, :], ybuf[:, :, :], tbuf[:, :, :],
                                          OP.add),
                                      waits=[t_cm_pend, t_yacc], inc="sG")
                        t_tb_free = t_yacc
                        t_cm_pend = None
                    t_ab = [None] * G
                    t_h = None
                    for g in range(G):
                        col = n * G + g
                        t_ab[g] = P.op("scalar",
                                       lambda e, g=g, n=n, csl=csl: e.activation(
                                           psab[g % 2][:, :], dt_sb[:, g, :],
                                           AF.Exp,
                                           scale=asb[:, g * NT + n:g * NT + n + 1]),
                                       waits=[t_dt[g], t_a2, t_abr[g % 2],
                                              t_abw[g % 2]], inc="sC")
                        t_abw[g % 2] = t_ab[g]
                        t_h = P.op("vector",
                                   lambda e, g=g, col=col, sl=sl: e.tensor_tensor_scan(
                                       hbuf[:, g, :], psab[g % 2][:, :],
                                       ubuf[:, sl, g, :],
                                       carry[:, col:col + 1], OP.mult, OP.add),
                                   waits=[t_ab[g], t_uh[g // GH], t_carryn[n], t_cm],
                                   inc="sV")
                        t_abr[g % 2] = t_h
                    t_scanned = t_h
                    t_scan_hist[sl] = t_h
                    if n == NT - 1:
                        t_dt_free[0] = t_ab[G - 1]
                    t_carryn[n] = P.op("scalar",
                                       lambda e, n=n: e.copy(
                                           carry[:, n * G:(n + 1) * G],
                                           hbuf[:, :, CW - 1:CW]
                                           .rearrange("p g one -> p (g one)")),
                                       waits=[t_scanned], inc="sC")
                    if n == 0:
                        t_cm = P.op("vector",
                                    lambda e, sl=sl: e.tensor_tensor(
                                        ybuf[:, :, :], hbuf[:, :, :], bcp(pscb[sl]),
                                        OP.mult),
                                    waits=[t_cbp, t_ybuf_free, t_scanned], inc="sV")
                        t_yacc = t_cm
                        t_cbr[sl] = t_cm
                        t_cm_pend = None
                    else:
                        t_cm = P.op("vector",
                                    lambda e, sl=sl: e.tensor_tensor(
                                        tbuf[:, :, :], hbuf[:, :, :], bcp(pscb[sl]),
                                        OP.mult),
                                    waits=[t_cbp, t_tb_free, t_scanned], inc="sV")
                        t_cbr[sl] = t_cm
                        t_cm_pend = t_cm
                    t_b_free[sl] = t_uh[0]
                    t_c_free[sl] = t_cm
                if t_cm_pend is not None:
                    t_yacc = P.op("gpsimd",
                                  lambda e: e.tensor_tensor(
                                      ybuf[:, :, :], ybuf[:, :, :], tbuf[:, :, :],
                                      OP.add),
                                  waits=[t_cm_pend, t_yacc], inc="sG")
                    t_cm_pend = None
                if c + 1 < NCH:
                    te_ = [None] * G
                    tm_ = [None] * G
                    for g in range(G):
                        te_[g] = P.op("scalar",
                                      lambda e, g=g, c=c: e.activation(
                                          dt_sb[:, g, :], d0p[:, (c + 1) % 2, :],
                                          AF.Exp,
                                          scale=vsb["dtw"][:, g:g + 1],
                                          bias=vsb["dtb"][:, g:g + 1]),
                                      waits=[t_d0l[c + 1], t_ld["dtw"],
                                             t_ld["dtb"], t_dt_free[0]], inc="sC")
                    for g in range(G):
                        tm_[g] = P.op("vector",
                                      lambda e, g=g: e.tensor_scalar(
                                          dt_sb[:, g, :], dt_sb[:, g, :], 1.6e19,
                                          None, OP.min),
                                      waits=[te_[g]], inc="sV")
                    tl_ = [None] * G
                    for g in range(G):
                        tl_[g] = P.op("scalar",
                                      lambda e, g=g: e.activation(
                                          dt_sb[:, g, :], dt_sb[:, g, :], AF.Ln,
                                          bias=1.0),
                                      waits=[tm_[g]], inc="sC")
                    t_dtld[c + 1] = tl_
                    t_d0_free[(c + 1) % 2] = te_[G - 1]
                t_y2s = [None] * G
                for g in range(G):
                    t_y2s[g] = P.op("vector",
                                    lambda e, g=g, csl=csl: e.scalar_tensor_tensor(
                                        y2c[:, g, :], xcc[:, csl, g, :],
                                        vsb["dv"][:, g:g + 1], ybuf[:, g, :],
                                        OP.mult, OP.add),
                                    waits=[t_yacc, t_xcl[c], t_ld["dv"],
                                           t_y2c_free], inc="sV")
                t_y2 = P.op("vector",
                            lambda e: e.tensor_tensor(y2c[:, :, :], y2c[:, :, :],
                                                      szz[:, :, :], OP.mult),
                            waits=[t_y2s[G - 1], t_sz], inc="sV")
                t_ybuf_free = t_y2s[G - 1]
                t_xcc_free[csl] = [t_y2s[G - 1]]
                t_szz_free = t_y2
                load_chunk(c + 2)
                wo_pend.append((c, t_y2))
                if c == NCH - 1:
                    for m in range(G):
                        emit_wout(c, t_y2, m)
                    t_y2c_free = t_wo_mm
                t_dtx_free = t_uh[1]
            t_rs = []
            for cp in range(4):
                t = P.op("gpsimd",
                         lambda e, cp=cp: e.collective_compute(
                             "ReduceScatter", OP.add, replica_groups=PAIRS,
                             ins=[opin[cp].opt()], outs=[opout[cp].opt()]),
                         waits=[("dOP1", P.cnt["dOP1"], "_dma_", -1),
                                ("dOP2", P.cnt["dOP2"], "_dma_", -1)], inc="sCC")
                t_rs.append(t)
            P.barrier()

        with ExitStack() as p3:
            x1 = SB("x1", [128, G, HALF], stack=p3)
            red8 = SB("red8", [128, 12], stack=p3)
            redg = SB("redg", [1, 128], stack=p3)
            sp1 = SB("sp1", [128, FT], stack=p3)
            bq1 = SB("bq1", [128, FT], stack=p3)
            sp2 = SB("sp2", [128, G], stack=p3)
            bq2 = SB("bq2", [128, G], stack=p3)
            xq1 = SB("xq1", [128, G, HALF], BF16, stack=p3)
            w1t_sb = SB("w1t_sb", [128, 2, G, 128], BF16, stack=p3)
            hst = SB("hst", [128, 4, CW], stack=p3)
            p3ps = p3.enter_context(ExitStack())
            psA = PS("psA3", [128, CW], stack=p3ps)
            psB = PS("psB3", [128, CW], stack=p3ps)
            psC = PS("psC3", [128, CW], stack=p3ps)
            psD = PS("psD3", [128, CW], stack=p3ps)
            psF1 = [psA, psB, psC, psD]
            p3a = p3.enter_context(ExitStack())
            psSX = PS("psSX3", [1, CW], stack=p3a)
            psSX2 = PS("psSX23", [1, CW], stack=p3a)
            mu2b = SB("mu2b", [128, HALF], stack=p3a)
            rs2b = SB("rs2b", [128, HALF], stack=p3a)
            tmp1 = SB("tmp1", [128, HALF], stack=p3a)
            mu1bh = SB("mu1bh", [128, HALF], stack=p3a)
            rs1bh = SB("rs1bh", [128, HALF], stack=p3a)
            tmp2 = SB("tmp2", [128, HALF], stack=p3a)
            sq_sb = SB("sq_sb", [128, 2, CW], stack=p3a)
            strow = SB("strow", [1, 7 * CW], stack=p3a)

            def r3(i):
                return strow[:, i * CW:(i + 1) * CW]

            def ffn_stats(src_getter, t_src, n_half, mu_plane, rs_plane,
                          t_prev=None, t_rowf=()):
                t_pst = t_prev
                t_sqr = [None, None]
                toks = []
                for hc in range(n_half // CW):
                    hs = slice(hc * CW, (hc + 1) * CW)
                    t_sq = [None, None]
                    t_mmx = None
                    t_mmx2 = None
                    for g in range(G):
                        sl = g % 2
                        t_sq[sl] = P.op("vector",
                                        lambda e, g=g, sl=sl, hs=hs: e.tensor_tensor(
                                            sq_sb[:, sl, :], src_getter(g)[:, hs],
                                            src_getter(g)[:, hs], OP.mult),
                                        waits=[t_src[g], t_sqr[sl]], inc="sV")
                        t_mmx = P.op("tensor",
                                     lambda e, g=g, hs=hs: e.matmul(
                                         psSX[:, :], ones_sb[:, :],
                                         src_getter(g)[:, hs], start=(g == 0),
                                         stop=(g == G - 1)),
                                     waits=[t_src[g], t_pst if g == 0 else None],
                                     inc="sT")
                        t_mmx2 = P.op("tensor",
                                      lambda e, g=g, sl=sl: e.matmul(
                                          psSX2[:, :], ones_sb[:, :], sq_sb[:, sl, :],
                                          start=(g == 0), stop=(g == G - 1)),
                                      waits=[t_sq[sl], t_pst if g == 0 else None],
                                      inc="sT")
                        t_sqr[sl] = t_mmx2
                    t_mu = P.op("vector",
                                lambda e: e.tensor_scalar(r3(0), psSX[:, :], INV_D,
                                                          None, OP.mult),
                                waits=[t_mmx] + list(t_rowf), inc="sV")
                    t_e2 = P.op("vector",
                                lambda e: e.tensor_scalar(r3(1), psSX2[:, :], INV_D,
                                                          None, OP.mult),
                                waits=[t_mmx2], inc="sV")
                    t_m2 = P.op("vector",
                                lambda e: e.tensor_tensor(r3(2), r3(0), r3(0),
                                                          OP.mult),
                                waits=[t_mu], inc="sV")
                    t_v1 = P.op("vector",
                                lambda e: e.tensor_tensor(r3(3), r3(1), r3(2),
                                                          OP.subtract),
                                waits=[t_e2, t_m2], inc="sV")
                    t_ln = P.op("scalar",
                                lambda e: e.activation(r3(2), r3(3), AF.Ln,
                                                       bias=EPS_LN),
                                waits=[t_v1], inc="sC")
                    t_rs1 = P.op("scalar",
                                 lambda e: e.activation(r3(4), r3(2), AF.Exp,
                                                        scale=-0.5),
                                 waits=[t_ln], inc="sC")
                    t_pst = t_e2
                    t_smu = P.dma("dRow", mu2row_h[:, hs], r3(0),
                                  waits=[t_m2] + list(t_rowf))
                    t_srs = P.dma("dRow", rst2row_h[:, hs], r3(4), waits=[t_rs1])
                    t_rowf = (t_smu, t_srs)
                    t_bmu = P.dma("dRow", mu_plane[:, hs],
                                  mu2row_h[:, hs].squeeze(0).partition_broadcast(128),
                                  waits=[t_smu])
                    t_brs = P.dma("dRow", rs_plane[:, hs],
                                  rst2row_h[:, hs].squeeze(0).partition_broadcast(128),
                                  waits=[t_srs])
                    toks.append((t_bmu, t_brs))
                return toks[-1][0], toks[-1][1], t_mmx2, t_rowf

            t_xm0 = P.dma("dX", x1[:, :, :],
                          xmine.rearrange("(g p) t -> p g t", p=128))
            t_xm = [t_xm0] * G
            t_pl = ffn_stats(lambda g: x1[:, g, :], t_xm, HALF, mu1bh, rs1bh)
            t_st1_te = t_pl[2]
            t_x1 = [None] * G
            t_tmp1_free = None
            t_tmp2_free = None
            for g in range(G):
                t1 = P.op("vector",
                          lambda e, g=g: e.tensor_tensor(tmp1[:, :], x1[:, g, :],
                                                         mu1bh[:, :], OP.subtract),
                          waits=[t_xm0, t_pl[0], t_tmp1_free], inc="sV")
                t2 = P.op("vector",
                          lambda e, g=g: e.tensor_tensor(tmp1[:, :], tmp1[:, :],
                                                         rs1bh[:, :], OP.mult),
                          waits=[t1, t_pl[1]], inc="sV")
                t3 = P.op("vector",
                          lambda e, g=g: e.tensor_tensor(x1[:, g, :], x1[:, g, :],
                                                         tmp1[:, :], OP.add),
                          waits=[t2, t_st1_te], inc="sV")
                t_tmp1_free = t3
                t_op = []
                for cp in range(4):
                    t_op.append(P.dma("aOP1" if cp % 2 == 0 else "aOP2", tmp2[:, cp * CW:(cp + 1) * CW],
                                      opout[cp][g * 128:(g + 1) * 128, :],
                                      waits=[t_rs[cp], t_tmp2_free]))
                t_x1[g] = P.op("vector",
                               lambda e, g=g: e.tensor_tensor(x1[:, g, :], x1[:, g, :],
                                                              tmp2[:, :], OP.add),
                               waits=[t3] + t_op, inc="sV")
                t_tmp2_free = t_x1[g]

            t_pl2 = ffn_stats(lambda g: x1[:, g, :], t_x1, HALF, mu2b, rs2b,
                              t_prev=t_pl[2], t_rowf=t_pl[3])

            t_red = None
            for g in range(G):
                t1 = P.op("vector",
                          lambda e, g=g: e.tensor_tensor(tmp1[:, :], x1[:, g, :],
                                                         mu2b[:, :], OP.subtract),
                          waits=[t_x1[g], t_pl2[0], t_tmp1_free], inc="sV")
                t2 = P.op("vector",
                          lambda e, g=g: e.tensor_tensor(tmp1[:, :], tmp1[:, :],
                                                         rs2b[:, :], OP.mult),
                          waits=[t1, t_pl2[1]], inc="sV")
                t_red = P.op("vector",
                             lambda e, g=g: e.tensor_reduce(red8[:, g:g + 1],
                                                            tmp1[:, :], AXX, OP.max),
                             waits=[t2], inc="sV")
                t_tmp1_free = t_red
            t_red = P.op("vector",
                         lambda e: e.tensor_reduce(red8[:, 8:9], red8[:, 0:8], AXX,
                                                   OP.max),
                         waits=[t_red], inc="sV")
            t_gv = P.dma("dMx", vbnc, red8[:, 8:9], waits=[t_red])
            t_gl = P.dma("dMx", redg[:, :], vbnc.rearrange("p one -> one p"),
                         waits=[t_gv])
            t_mx = P.op("vector",
                        lambda e: e.tensor_reduce(srow[:, 0:1], redg[:, :], AXX, OP.max),
                        waits=[t_gl], inc="sV")
            t_mxe = P.op("vector",
                         lambda e: e.tensor_scalar(srow[:, 0:1], srow[:, 0:1], EPS_Q,
                                                   None, OP.max),
                         waits=[t_mx], inc="sV")
            t_mxd = P.dma("gB", mx_in, srow[:, 0:1], waits=[t_mxe])
            t_armx = P.op("gpsimd",
                          lambda e: e.collective_compute(
                              "AllReduce", OP.max, replica_groups=ALL8,
                              ins=[mx_in.opt()], outs=[mx_out.opt()]),
                          waits=[t_mxd], inc="sCC")
            t_m1b = P.dma("dMx", scal[:, 0:1],
                          mx_out.squeeze(0).partition_broadcast(128), waits=[t_armx])
            t_s1 = P.op("vector",
                        lambda e: e.tensor_scalar(scal[:, 1:2], scal[:, 0:1],
                                                  1.0 / 255.0, None, OP.mult),
                        waits=[t_m1b], inc="sV")
            t_r1 = P.op("vector", lambda e: e.reciprocal(scal[:, 2:3], scal[:, 1:2]),
                        waits=[t_s1], inc="sV")
            t_sp1 = P.op("vector",
                         lambda e: e.tensor_scalar(sp1[:, :], sw1[:, :], scal[:, 1:2],
                                                   None, OP.mult),
                         waits=[t_s1], inc="sV")
            t_rq = P.op("vector", lambda e: e.reciprocal(bq1[:, :], sp1[:, :]),
                        waits=[t_sp1], inc="sV")
            t_b1 = P.op("vector",
                        lambda e: e.tensor_tensor(bq1[:, :], vsb["b1v"][:, :], bq1[:, :],
                                                  OP.mult),
                        waits=[t_rq, t_ld["b1v"]], inc="sV")
            t_b1r = P.op("vector",
                         lambda e: e.tensor_scalar(bq1[:, :], bq1[:, :], MAGIC, MAGIC,
                                                   OP.add, OP.subtract),
                         waits=[t_b1], inc="sV")
            t_bq1 = P.op("vector",
                         lambda e: e.tensor_tensor(bq1[:, :], bq1[:, :], sp1[:, :],
                                                   OP.mult),
                         waits=[t_b1r], inc="sV")

            t_xq1 = [None] * G
            for g in range(G):
                t1 = P.op("vector",
                          lambda e, g=g: e.tensor_tensor(tmp1[:, :], x1[:, g, :],
                                                         mu2b[:, :], OP.subtract),
                          waits=[t_x1[g], t_tmp1_free], inc="sV")
                t2 = P.op("vector",
                          lambda e, g=g: e.tensor_tensor(tmp1[:, :], tmp1[:, :],
                                                         rs2b[:, :], OP.mult),
                          waits=[t1], inc="sV")
                t3 = P.op("vector",
                          lambda e, g=g: e.tensor_scalar(tmp1[:, :], tmp1[:, :],
                                                         scal[:, 2:3], MAGIC,
                                                         OP.mult, OP.add),
                          waits=[t2, t_r1], inc="sV")
                t_xq1[g] = P.op("vector",
                                lambda e, g=g: e.tensor_scalar(xq1[:, g, :], tmp1[:, :],
                                                               MAGIC, 0.0,
                                                               OP.subtract, OP.max),
                                waits=[t3], inc="sV")
                t_tmp1_free = t_xq1[g]

            P.barrier()
            p3a.close()
            wq2a = SB("wq2a", [128, FT, DM], BF16, stack=p3)
            t_w2a = [None] * 4
            for qc in range(4):
                t_w2a[qc] = P.dma("aW2" if qc % 2 == 0 else "aW",
                                  wq2a[:, qc * G:(qc + 1) * G, :],
                                  w2q_h.rearrange("(f p) d -> p f d", p=128)
                                  [:, qc * G:(qc + 1) * G, :])

            t_w1_free = [None, None]
            t_ps_free = [None, None, None, None]
            t_hst_free = [[], [], [], []]
            t_hmax = P.op("vector", lambda e: e.memset(red8[:, 9:10], 0.0), inc="sV")
            t_hd = None
            for ft in range(FT):
                sl = ft % 2
                t_w = P.dma("aW1", w1t_sb[:, sl, :, :],
                            w1q_h.rearrange("(g p) f -> p g f", p=128)
                            [:, :, ft * 128:(ft + 1) * 128],
                            waits=[t_w1_free[sl]])
                t_mm_last = None
                for hc in range(4):
                    hs = slice(hc * CW, (hc + 1) * CW)
                    psl = (ft * 4 + hc) % 4
                    ps = psF1[psl]
                    t_mm = None
                    for g in range(G):
                        t_mm = P.op("tensor",
                                    lambda e, g=g, sl=sl, hs=hs, ps=ps: e.matmul(
                                        ps[:, :], w1t_sb[:, sl, g, :], xq1[:, g, hs],
                                        start=(g == 0), stop=(g == G - 1)),
                                    waits=[t_w, t_xq1[g],
                                           t_ps_free[psl] if g == 0 else None],
                                    inc="sT")
                    t_hv = P.op("scalar",
                                lambda e, ft=ft, psl=psl, ps=ps: e.activation(
                                    hst[:, psl, :], ps[:, :], AF.Relu,
                                    scale=sp1[:, ft:ft + 1], bias=bq1[:, ft:ft + 1]),
                                waits=[t_mm, t_sp1, t_bq1]
                                      + (t_hst_free[psl] or []), inc="sC")
                    t_ps_free[psl] = t_hv
                    t_hd = P.dma("dH%d" % ((ft * 4 + hc) % 4 + 1),
                                 h_h.rearrange("(f p) t -> p f t", p=128)[:, ft, hs],
                                 hst[:, psl, :], waits=[t_hv])
                    t_hm = P.op("vector",
                                lambda e, psl=psl: e.tensor_reduce(
                                    red8[:, 10:11], hst[:, psl, :], AXX, OP.max),
                                waits=[t_hv, t_hmax], inc="sV")
                    t_hmax = P.op("vector",
                                  lambda e: e.tensor_tensor(red8[:, 9:10], red8[:, 9:10],
                                                            red8[:, 10:11], OP.max),
                                  waits=[t_hm], inc="sV")
                    t_hst_free[psl] = [t_hd, t_hm]
                    t_mm_last = t_mm
                t_w1_free[sl] = t_mm_last
            t_gv2 = P.dma("dMx", vbnc, red8[:, 9:10], waits=[t_hmax])
            t_gl2 = P.dma("dMx", redg[:, :], vbnc.rearrange("p one -> one p"),
                          waits=[t_gv2])
            t_mx2 = P.op("vector",
                         lambda e: e.tensor_reduce(srow[:, 1:2], redg[:, :], AXX,
                                                   OP.max),
                         waits=[t_gl2], inc="sV")
            t_mxe2 = P.op("vector",
                          lambda e: e.tensor_scalar(srow[:, 1:2], srow[:, 1:2], EPS_Q,
                                                    None, OP.max),
                          waits=[t_mx2], inc="sV")
            t_mxd2 = P.dma("gB", mx2_in, srow[:, 1:2], waits=[t_mxe2])
            t_armx2 = P.op("gpsimd",
                           lambda e: e.collective_compute(
                               "AllReduce", OP.max, replica_groups=ALL8,
                               ins=[mx2_in.opt()], outs=[mx2_out.opt()]),
                           waits=[t_mxd2], inc="sCC")
            t_m2b = P.dma("dMx", scal[:, 3:4],
                          mx2_out.squeeze(0).partition_broadcast(128), waits=[t_armx2])
            t_s2 = P.op("vector",
                        lambda e: e.tensor_scalar(scal[:, 4:5], scal[:, 3:4],
                                                  1.0 / 255.0, None, OP.mult),
                        waits=[t_m2b], inc="sV")
            t_r2 = P.op("vector", lambda e: e.reciprocal(scal[:, 5:6], scal[:, 4:5]),
                        waits=[t_s2], inc="sV")
            t_sp2 = P.op("vector",
                         lambda e: e.tensor_scalar(sp2[:, :], sw2[:, :], scal[:, 4:5],
                                                   None, OP.mult),
                         waits=[t_s2], inc="sV")
            t_rq2 = P.op("vector", lambda e: e.reciprocal(bq2[:, :], sp2[:, :]),
                         waits=[t_sp2], inc="sV")
            t_b2a = P.op("vector",
                         lambda e: e.tensor_tensor(bq2[:, :], vsb["b2v"][:, :],
                                                   bq2[:, :], OP.mult),
                         waits=[t_rq2, t_ld["b2v"]], inc="sV")
            t_b2b = P.op("vector",
                         lambda e: e.tensor_scalar(bq2[:, :], bq2[:, :], MAGIC, MAGIC,
                                                   OP.add, OP.subtract),
                         waits=[t_b2a], inc="sV")
            t_bq2 = P.op("vector",
                         lambda e: e.tensor_tensor(bq2[:, :], bq2[:, :], sp2[:, :],
                                                   OP.mult),
                         waits=[t_b2b], inc="sV")

            P.barrier()
            p3ps.close()
            psm = [PS("psm%d" % i, [128, CW], stack=p3) for i in range(G)]
            xq2 = SB("xq2", [128, FT, CW], BF16, stack=p3)
            hld = hst
            ost = hst[:, 2:4, :]
            t_hall = [("dH%d" % i, P.cnt["dH%d" % i], "_dma_", -1)
                      for i in (1, 2, 3, 4)]
            t_out_last = None
            t_hl_free = [None, None]
            t_xq2_free = [None] * FT
            t_ost_free = [None, None]
            t_psm_free = [None] * G
            for hc in range(4):
                hs = slice(hc * CW, (hc + 1) * CW)
                t_xq2t = [None] * FT
                for ft in range(FT):
                    sl = ft % 2
                    t_hl = P.dma("aH%d" % (ft % 4 + 1), hld[:, sl, :],
                                 h_h.rearrange("(f p) t -> p f t", p=128)[:, ft, hs],
                                 waits=t_hall + [t_hl_free[sl]])
                    t1 = P.op("vector",
                              lambda e, sl=sl: e.tensor_scalar(
                                  hld[:, sl, :], hld[:, sl, :], scal[:, 5:6], MAGIC,
                                  OP.mult, OP.add),
                              waits=[t_hl, t_r2], inc="sV")
                    t_xq2t[ft] = P.op("vector",
                                      lambda e, ft=ft, sl=sl: e.tensor_scalar(
                                          xq2[:, ft, :], hld[:, sl, :], MAGIC, None,
                                          OP.subtract),
                                      waits=[t1, t_xq2_free[ft]], inc="sV")
                    t_hl_free[sl] = t_xq2t[ft]
                t_mm = None
                for ftg in range(4):
                    for m in range(G):
                        for fl in range(G):
                            ft = ftg * G + fl
                            t_mm = P.op("tensor",
                                        lambda e, ft=ft, m=m,
                                        ps=psm[m]: e.matmul(
                                            ps[:, :],
                                            wq2a[:, ft, m * 128:(m + 1) * 128],
                                            xq2[:, ft, :],
                                            start=(ft == 0),
                                            stop=(ft == FT - 1)),
                                        waits=[t_xq2t[ft], t_w2a[ftg],
                                               t_psm_free[m] if ft == 0
                                               else None],
                                        inc="sT")
                for m in range(G):
                    psl = m % 2
                    t_o1 = P.op("vector",
                                lambda e, m=m, psl=psl, ps=psm[m]: e.tensor_scalar(
                                    ost[:, psl, :], ps[:, :], sp2[:, m:m + 1],
                                    bq2[:, m:m + 1], OP.mult, OP.add),
                                waits=[t_mm, t_sp2, t_bq2, t_ost_free[psl]], inc="sV")
                    t_psm_free[m] = t_o1
                    t_o2 = P.op("vector",
                                lambda e, m=m, psl=psl, hs=hs: e.tensor_tensor(
                                    ost[:, psl, :], ost[:, psl, :], x1[:, m, hs],
                                    OP.add),
                                waits=[t_o1], inc="sV")
                    t_out_last = P.dma("dOut1" if m % 2 == 0 else "dOut2",
                                       out_h[m * 128:(m + 1) * 128, hs],
                                       ost[:, psl, :], waits=[t_o2])
                    t_ost_free[psl] = t_out_last
                for ft in range(FT):
                    t_xq2_free[ft] = t_mm
            P.op("sync", None, waits=[t_out_last])

        block = es.enter_context(nc.Block())
        P.replay(block)
    return nc


_NC_CACHE = None


def kernel(L0_attn_x, L0_attn_norm_w, L0_attn_norm_b, L0_attn_W_in, L0_attn_conv_w,
           L0_attn_conv_b, L0_attn_x_proj_w, L0_attn_dt_w, L0_attn_dt_b, L0_attn_A_log,
           L0_attn_D, L0_attn_W_out, L0_ffn_w1, L0_ffn_b1, L0_ffn_w2, L0_ffn_b2):
    global _NC_CACHE
    f32 = np.float32
    x = np.asarray(L0_attn_x, f32)
    W_in = np.asarray(L0_attn_W_in, f32)
    conv_w = np.asarray(L0_attn_conv_w, f32)
    ident = np.eye(128).astype(np.float32)
    import ml_dtypes
    ident_bf = ident.astype(ml_dtypes.bfloat16)

    def v2d(v, g):
        return np.ascontiguousarray(np.asarray(v, f32).reshape(g, 128).T)

    in_maps = []
    for k in range(8):
        b, j = k // 2, k % 2
        ch = slice(j * 1024, (j + 1) * 1024)
        m = {
            "xT": np.ascontiguousarray(x[b].T),
            "xmine": np.ascontiguousarray(x[b][j * HALF:(j + 1) * HALF].T),
            "winm": np.ascontiguousarray(
                np.concatenate([W_in[:, ch],
                                W_in[:, 2048 + j * 1024:2048 + (j + 1) * 1024]],
                               axis=1)),
            "xpw": np.ascontiguousarray(np.asarray(L0_attn_x_proj_w, f32)[ch]),
            "woutm": np.ascontiguousarray(np.asarray(L0_attn_W_out, f32)[ch]),
            "w1n": np.ascontiguousarray(np.asarray(L0_ffn_w1, f32)),
            "w2n": np.ascontiguousarray(np.asarray(L0_ffn_w2, f32)),
            "alog2d": np.ascontiguousarray(
                np.asarray(L0_attn_A_log, f32)[ch].reshape(8, 128, 16)
                .transpose(1, 0, 2).reshape(128, 128)),
            "ident": ident_bf,
            "cw0": v2d(conv_w[ch, 0], 8),
            "cw1": v2d(conv_w[ch, 1], 8),
            "cw2": v2d(conv_w[ch, 2], 8),
            "cw3": v2d(conv_w[ch, 3], 8),
            "cb": v2d(np.asarray(L0_attn_conv_b, f32)[ch], 8),
            "dv": v2d(np.asarray(L0_attn_D, f32)[ch], 8),
            "dtw": v2d(np.asarray(L0_attn_dt_w, f32)[ch], 8),
            "dtb": v2d(np.asarray(L0_attn_dt_b, f32)[ch], 8),
            "nw": v2d(np.asarray(L0_attn_norm_w, f32), 8),
            "nb": v2d(np.asarray(L0_attn_norm_b, f32), 8),
            "b1v": v2d(np.asarray(L0_ffn_b1, f32), 32),
            "b2v": v2d(np.asarray(L0_ffn_b2, f32), 8),
        }
        in_maps.append(m)

    if _NC_CACHE is None:
        _NC_CACHE = build_nc()
    res = run_bass_kernel_spmd(_NC_CACHE, in_maps, core_ids=list(range(8)))

    y = np.empty((4, L, DM), f32)
    for k in range(8):
        b, j = k // 2, k % 2
        y[b, j * HALF:(j + 1) * HALF, :] = np.asarray(res.results[k]["out"]).T
    return y


# version 23
# speedup vs baseline: 1.0184x; 1.0184x over previous
import numpy as np
from contextlib import ExitStack

import concourse.bass as bass
import concourse.mybir as mybir
from concourse.bass_utils import run_bass_kernel_spmd

F32 = mybir.dt.float32
BF16 = mybir.dt.bfloat16
OP = mybir.AluOpType
AF = mybir.ActivationFunctionType
AXX = mybir.AxisListType.X

MAGIC = 12582912.0
DM = 1024
G = 8
NT = 16
L = 4096
CW = 512
NCH = L // CW
HALF = L // 2
F = 4096
FT = F // 128
EPS_LN = 1e-5
EPS_Q = 1e-8
INV_D = 1.0 / 1024.0
PAIRS = [[0, 1], [2, 3], [4, 5], [6, 7]]
ALL8 = [list(range(8))]


class Prog:

    STREAMS = {
        "dV": "sync", "dX": "sync", "dW": "sync", "dRow": "sync",
        "dS1": "sync", "dS2": "sync", "dOP1": "sync", "dOP2": "sync",
        "dH1": "sync", "dH2": "sync", "dH3": "sync", "dH4": "sync",
        "dOut1": "sync", "dOut2": "sync", "dMx": "sync",
        "aW": "scalar", "aWs1": "scalar", "aWs2": "scalar",
        "aB1": "scalar", "aB2": "scalar",
        "aC1": "scalar", "aC2": "scalar", "aD0": "scalar",
        "aXC": "scalar", "aSZ": "scalar", "aOP1": "scalar", "aOP2": "scalar",
        "aH1": "scalar", "aH2": "scalar", "aH3": "scalar", "aH4": "scalar",
        "aW1": "scalar", "aW2": "scalar",
        "gB": "gpsimd",
    }

    def __init__(self, nc, es):
        self.nc = nc
        self.ops = {e: [] for e in ("sync", "scalar", "vector", "gpsimd", "tensor")}
        self.sems = {}
        self.cnt = {}
        self.pos = {e: 0 for e in self.ops}
        self.last_drain = {e: -1 for e in self.ops}
        self.waited = {e: {} for e in self.ops}
        names = (["sT", "sV", "sC", "sG", "sCC", "bS", "bA", "bV", "bG", "bT"]
                 + list(self.STREAMS))
        for s in names:
            self.sems[s] = es.enter_context(nc.semaphore(s))
            self.cnt[s] = 0

    def op(self, eng, fn, waits=(), inc=None, inc_n=1):
        my = self.ops[eng]
        nw = []
        need_drain = False
        for t in waits:
            if t is None:
                continue
            s, v, peng, ppos = t
            if peng == eng and s not in self.STREAMS:
                if ppos > self.last_drain[eng]:
                    need_drain = True
            else:
                if self.waited[eng].get(s, -1) < v:
                    nw.append((s, v))
                    self.waited[eng][s] = v
        if need_drain:
            my.append(("drain", None, None, None))
            self.last_drain[eng] = self.pos[eng]
            self.pos[eng] += 1
        tok = None
        if inc is not None:
            self.cnt[inc] += inc_n
            tok = (inc, self.cnt[inc], eng, self.pos[eng])
        my.append(("op", fn, nw, (inc, inc_n) if inc else None))
        self.pos[eng] += 1
        return tok

    def dma(self, stream, dst, src, waits=()):
        q = self.STREAMS[stream]
        waits = list(waits)
        if self.cnt[stream] > 0:
            waits.append((stream, self.cnt[stream], "_dma_", -1))
        return self.op(q, lambda e, d=dst, s=src: e.dma_start(out=d, in_=s),
                       waits=waits, inc=stream, inc_n=16)

    def barrier(self):
        bs = {"sync": "bS", "scalar": "bA", "vector": "bV",
              "gpsimd": "bG", "tensor": "bT"}
        dw = [(s, self.cnt[s], "_dma_", -1) for s in self.STREAMS
              if self.cnt[s] > 0]
        toks = []
        for e in self.ops:
            self.ops[e].append(("drain", None, None, None))
            self.last_drain[e] = self.pos[e]
            self.pos[e] += 1
            toks.append(self.op(e, lambda eng: eng.nop(),
                                waits=[t for t in dw], inc=bs[e]))
        for e in self.ops:
            self.op(e, None, waits=[t for t in toks if t[2] != e])
            self.last_drain[e] = self.pos[e] - 1
        return toks

    def replay(self, block):
        def mk(eng_name):
            def body(eng):
                for kind, fn, waits, inc in self.ops[eng_name]:
                    if kind == "drain":
                        eng.drain()
                        continue
                    for s, v in waits:
                        eng.wait_ge(self.sems[s], v)
                    if fn is None:
                        continue
                    instr = fn(eng)
                    if inc is not None and instr is not None:
                        instr.then_inc(self.sems[inc[0]], inc[1])
            return body
        block.sync(mk("sync"))
        block.scalar(mk("scalar"))
        block.vector(mk("vector"))
        block.gpsimd(mk("gpsimd"))
        block.tensor(mk("tensor"))


def build_nc():
    nc = bass.Bass("TRN2", target_bir_lowering=False, debug=False)

    def dr_in(name, shape, dt=F32):
        return nc.dram_tensor(name, shape, dt, kind="ExternalInput").ap()

    xT = dr_in("xT", [DM, L])
    xmine = dr_in("xmine", [DM, HALF])
    winm = dr_in("winm", [DM, 2048])
    xpw = dr_in("xpw", [1024, 33])
    woutm = dr_in("woutm", [1024, DM])
    w1n = dr_in("w1n", [F, DM])
    w2n = dr_in("w2n", [DM, F])
    alog2d = dr_in("alog2d", [128, 128])
    ident_in = dr_in("ident", [128, 128], BF16)
    vec_shapes = {"cw0": G, "cw1": G, "cw2": G, "cw3": G, "cb": G, "dv": G,
                  "dtw": G, "dtb": G, "nw": G, "nb": G, "b1v": FT, "b2v": G}
    vec = {nm: dr_in(nm, [128, g]) for nm, g in vec_shapes.items()}
    out_h = nc.dram_tensor("out", [DM, HALF], F32, kind="ExternalOutput").ap()

    def dr(name, shape, dt=F32):
        return nc.dram_tensor(name, shape, dt).ap()

    xc_h = dr("xc_h", [1024, L])
    sz_h = dr("sz_h", [1024, L])
    dbc_bnc = dr("dbc_bnc", [33, L])
    dbc_red = dr("dbc_red", [33, L])
    murow_h = dr("murow_h", [1, L])
    s2row_h = dr("s2row_h", [1, 4096])
    dt_hbm = dr("dt_hbm", [1024, L])
    rrow_h = dr("rrow_h", [1, L])
    w1q_h = dr("w1q_h", [DM, F], BF16)
    w2q_h = dr("w2q_h", [F, DM], BF16)
    h_h = dr("h_h", [F, HALF])
    opin = [dr(f"opin{i}", [2, DM, CW]) for i in range(4)]
    opout = [dr(f"opout{i}", [DM, CW]) for i in range(4)]
    mu2row_h = dr("mu2row_h", [1, HALF])
    rst2row_h = dr("rst2row_h", [1, HALF])
    vbnc = dr("vbnc", [128, 1])
    mx_in = dr("mx_in", [1, 1])
    mx_out = dr("mx_out", [1, 1])
    mx2_in = dr("mx2_in", [1, 1])
    mx2_out = dr("mx2_out", [1, 1])

    es = ExitStack()
    with es:
        P = Prog(nc, es)

        def SB(name, shape, dt=F32, stack=None):
            return (stack or es).enter_context(nc.sbuf_tensor(name, shape, dt))

        def PS(name, shape, dt=F32, stack=None):
            return (stack or es).enter_context(nc.psum_tensor(name, shape, dt))

        cst_sb = SB("cst_sb", [128, 4])
        for i, cv in enumerate([EPS_LN, 1.0]):
            nc.const_aps.register(F32, cv, cst_sb[:, i:i + 1])
        ones_sb = SB("ones_sb", [128, 1])
        ident = SB("ident_sb", [128, 128], BF16)
        asb = SB("asb", [128, 128])
        vsb = {nm: SB(nm + "_sb", [128, vec_shapes[nm]]) for nm in vec}
        carry = SB("carry", [128, 128])
        sw1 = SB("sw1", [128, FT])
        sw2 = SB("sw2", [128, G])
        scal = SB("scal", [128, 8])
        srow = SB("srow", [1, 16])


        P.op("vector", lambda e: e.memset(cst_sb[:, 0:1], EPS_LN), inc="sV")
        P.op("vector", lambda e: e.memset(cst_sb[:, 1:2], 1.0), inc="sV")
        t_ld = {nm: P.dma("dV", vsb[nm][:, :], vec[nm]) for nm in vec}
        t_alog = P.dma("dV", asb[:, :], alog2d)
        t_id = P.dma("dV", ident[:, :], ident_in)
        P.op("vector", lambda e: e.memset(ones_sb[:, :], 1.0), inc="sV")
        t_carry0 = P.op("vector", lambda e: e.memset(carry[:, :], 0.0), inc="sV")
        t_a1 = P.op("scalar", lambda e: e.activation(asb[:, :], asb[:, :], AF.Exp),
                    waits=[t_alog], inc="sC")
        t_a2 = P.op("vector",
                    lambda e: e.tensor_scalar(asb[:, :], asb[:, :], -1.0, None, OP.mult),
                    waits=[t_a1], inc="sV")

        q_st = {"pc": [None, None], "pst": [None, None], "tr_i": 0}
        st1 = {"q2": None, "qread": None}
        st2 = {"q2": None, "qread": None}

        with ExitStack() as p1:
            winm_sb = SB("winm_sb", [128, G, 2048], stack=p1)
            xpw_sb = SB("xpw_sb", [128, G, 33], stack=p1)
            xch = SB("xch", [128, 2, G, CW], stack=p1)
            xsq = SB("xsq", [128, 2, CW], stack=p1)
            pad = SB("pad", [128, G, 515], stack=p1)
            convo = SB("convo", [128, CW], stack=p1)
            xcs = SB("xcs", [128, 8, CW], stack=p1)
            szc = SB("szc", [128, 2, CW], stack=p1)
            evt = SB("evt", [128, 2, CW], stack=p1)
            rbp = SB("rbp", [128, 2, CW], stack=p1)
            rmbp = SB("rmbp", [128, 2, CW], stack=p1)
            dbcp = SB("dbcp", [33, 2, CW], stack=p1)
            s1c = SB("s1c", [128, 32], stack=p1)
            cbp2 = SB("cbp2", [128, 8], stack=p1)
            st_r = SB("st_r", [1, 7 * CW], stack=p1)
            psA = PS("psA", [128, CW], stack=p1)
            psB = PS("psB", [128, CW], stack=p1)
            psDBC = PS("psDBC", [33, CW], stack=p1)
            psSX = PS("psSX", [1, CW], stack=p1)
            psSX2 = PS("psSX2", [1, CW], stack=p1)
            psS12 = PS("psS12", [1, CW], stack=p1)

            wnat = SB("wnat", [128, DM], stack=p1)
            wnat2 = SB("wnat2", [128, F], stack=p1)
            wqn = SB("wqn", [128, DM], BF16, stack=p1)
            wqn2 = SB("wqn2", [128, F], BF16, stack=p1)
            tstage = SB("tstage", [128, 2, 8 * 128], BF16, stack=p1)
            rcol = SB("rcol", [128, 2], stack=p1)
            psTR = [PS("psTR0", [128, 1024], BF16, stack=p1),
                    PS("psTR1", [128, 1024], BF16, stack=p1)]

            def quant_tile(src_ap, dst_nat, dst_q, scol, sl, st):
                t_l = P.dma("aW", dst_nat, src_ap, waits=[st["q2"]])
                t_red = P.op("vector",
                             lambda e: e.tensor_reduce(scol, dst_nat, AXX, OP.max,
                                                       apply_absolute_value=True),
                             waits=[t_l], inc="sV")
                t_m = P.op("vector",
                           lambda e: e.tensor_scalar(scol, scol, EPS_Q, 1.0 / 127.0,
                                                     OP.max, OP.mult),
                           waits=[t_red], inc="sV")
                t_rc = P.op("vector",
                            lambda e: e.reciprocal(rcol[:, sl:sl + 1], scol),
                            waits=[t_m], inc="sV")
                t_q1 = P.op("vector",
                            lambda e: e.tensor_scalar(dst_nat, dst_nat,
                                                      rcol[:, sl:sl + 1], MAGIC,
                                                      OP.mult, OP.add),
                            waits=[t_rc], inc="sV")
                t_q2 = P.op("vector",
                            lambda e: e.tensor_scalar(dst_q, dst_nat, MAGIC,
                                                      None, OP.subtract),
                            waits=[t_q1, st["qread"]], inc="sV")
                st["q2"] = t_q2
                return t_q2

            def w1_A(ft):
                quant_tile(w1n[ft * 128:(ft + 1) * 128, :], wnat[:, :],
                           wqn[:, :], sw1[:, ft:ft + 1], ft % 2, st1)

            def w1_B(ft):
                t_q2 = st1["q2"]
                sl = ft % 2
                t_pcl = None
                t_tr = None
                for hg in range(2):
                    s4 = q_st["tr_i"] % 2
                    q_st["tr_i"] += 1
                    for q in range(4):
                        g = hg * 4 + q
                        t_tr = P.op("tensor",
                                    lambda e, g=g, s4=s4, q=q: e.transpose(
                                        psTR[s4][:, q * 256:q * 256 + 128],
                                        wqn[:, g * 128:(g + 1) * 128],
                                        ident[:, :]),
                                    waits=[t_q2, t_id,
                                           q_st["pc"][s4] if q == 0 else None],
                                    inc="sT")
                    q_st["pc"][s4] = P.op("scalar",
                                          lambda e, hg=hg, sl=sl, s4=s4: e.copy(
                                              tstage[:, sl, hg * 512:(hg + 1) * 512]
                                              .rearrange("p (q c) -> p q c", q=4),
                                              psTR[s4][:, :]
                                              .rearrange("p (q c) -> p q c", q=4)
                                              [:, :, 0:128]),
                                          waits=[t_tr, q_st["pst"][sl]], inc="sC")
                    t_pcl = q_st["pc"][s4]
                st1["qread"] = t_tr
                q_st["pst"][sl] = P.dma("aWs1" if sl == 0 else "aWs2",
                                        w1q_h.rearrange("(g p) f -> p g f", p=128)
                                        [:, :, ft * 128:(ft + 1) * 128],
                                        tstage[:, sl, :]
                                        .rearrange("p (g c) -> p g c", g=G),
                                        waits=[t_pcl])

            def w2_A(g):
                quant_tile(w2n[g * 128:(g + 1) * 128, :], wnat2[:, :],
                           wqn2[:, :], sw2[:, g:g + 1], g % 2, st2)

            def w2_B(g):
                t_q2 = st2["q2"]
                sl = g % 2
                t_tr = None
                for fh in range(8):
                    s4 = q_st["tr_i"] % 2
                    q_st["tr_i"] += 1
                    for q in range(4):
                        ft = fh * 4 + q
                        t_tr = P.op("tensor",
                                    lambda e, ft=ft, s4=s4, q=q: e.transpose(
                                        psTR[s4][:, q * 256:q * 256 + 128],
                                        wqn2[:, ft * 128:(ft + 1) * 128],
                                        ident[:, :]),
                                    waits=[t_q2,
                                           q_st["pc"][s4] if q == 0 else None],
                                    inc="sT")
                    hh = fh % 2
                    q_st["pc"][s4] = P.op("scalar",
                                          lambda e, hh=hh, sl=sl, s4=s4: e.copy(
                                              tstage[:, sl, hh * 512:(hh + 1) * 512]
                                              .rearrange("p (q c) -> p q c", q=4),
                                              psTR[s4][:, :]
                                              .rearrange("p (q c) -> p q c", q=4)
                                              [:, :, 0:128]),
                                          waits=[t_tr, q_st["pst"][sl]], inc="sC")
                    if hh == 1:
                        q_st["pst"][sl] = P.dma("aWs1" if fh % 4 == 1 else "aWs2",
                                                w2q_h.rearrange("(fg p) d -> p fg d",
                                                                p=128)
                                                [:, fh * 4 - 4:fh * 4 + 4,
                                                 g * 128:(g + 1) * 128],
                                                tstage[:, sl, :]
                                                .rearrange("p (s c) -> p s c", s=8),
                                                waits=[q_st["pc"][s4]])
                st2["qread"] = t_tr

            p0_units = ([(lambda ft=ft: w1_A(ft), lambda ft=ft: w1_B(ft))
                         for ft in range(FT)] +
                        [(lambda g=g: w2_A(g), lambda g=g: w2_B(g))
                         for g in range(G)])
            p0_i = [0]
            p0_a = [0]

            def pump_A():
                if p0_a[0] < len(p0_units) and p0_a[0] == p0_i[0]:
                    p0_units[p0_a[0]][0]()
                    p0_a[0] += 1

            def pump_B():
                if p0_i[0] < p0_a[0]:
                    p0_units[p0_i[0]][1]()
                    p0_i[0] += 1

            def row(i):
                return st_r[:, i * CW:(i + 1) * CW]

            t_winm = P.dma("dW", winm_sb[:, :, :],
                           winm.rearrange("(g p) c -> p g c", p=128))
            t_xpw = P.dma("dW", xpw_sb[:, :, :],
                          xpw.rearrange("(g p) c -> p g c", p=128))
            t_padz = P.op("vector", lambda e: e.memset(pad[:, :, 0:3], 0.0), inc="sV")

            t_s2m = None
            t_s1m = None
            t_wsc = [None] * G
            for cc in range(4):
                ccs = slice(cc * CW, (cc + 1) * CW)
                for g in range(G):
                    t_s2m = P.op("tensor",
                                 lambda e, g=g, ccs=ccs: e.matmul(
                                     psS12[:, :], vsb["nb"][:, g:g + 1],
                                     winm_sb[:, g, ccs], start=(g == 0),
                                     stop=(g == G - 1)),
                                 waits=[t_winm, t_ld["nb"],
                                        t_s2ev if cc > 0 else None], inc="sT")
                t_s2ev = P.op("vector",
                              lambda e, ccs=ccs: e.tensor_copy(st_r[:, ccs],
                                                               psS12[:, :]),
                              waits=[t_s2m], inc="sV")
            for g in range(G):
                t_wsc[g] = P.op("vector",
                                lambda e, g=g: e.tensor_scalar(
                                    winm_sb[:, g, :], winm_sb[:, g, :],
                                    vsb["nw"][:, g:g + 1], None, OP.mult),
                                waits=[t_winm, t_ld["nw"], t_s2ev], inc="sV")
            t_s2s = P.dma("dMx", s2row_h[:, 2048:4096], st_r[:, 0:4 * CW],
                          waits=[t_s2ev])
            t_s1ev = None
            for cc in range(4):
                ccs = slice(cc * CW, (cc + 1) * CW)
                for g in range(G):
                    t_s1m = P.op("tensor",
                                 lambda e, g=g, ccs=ccs: e.matmul(
                                     psS12[:, :], ones_sb[:, :],
                                     winm_sb[:, g, ccs], start=(g == 0),
                                     stop=(g == G - 1)),
                                 waits=[t_wsc[g], t_s2ev,
                                        t_s1ev if cc > 0 else None], inc="sT")
                t_s1ev = P.op("vector",
                              lambda e, cc=cc: e.tensor_copy(
                                  st_r[:, cc * CW:(cc + 1) * CW], psS12[:, :]),
                              waits=[t_s1m, t_s2s], inc="sV")
            t_s1s = P.dma("dMx", s2row_h[:, 0:2048], st_r[:, 0:4 * CW],
                          waits=[t_s1ev, t_s2s])
            t_s1l = P.op("sync",
                         lambda e: e.dma_start(
                             out=s1c[:, :],
                             in_=s2row_h.rearrange("one (c p) -> p (one c)", p=128),
                             allow_slow_non_contiguous=True),
                         waits=[("dMx", P.cnt["dMx"], "_dma_", -1)],
                         inc="dMx", inc_n=16)
            t_s1n = P.op("vector",
                         lambda e: e.tensor_scalar(s1c[:, 0:16], s1c[:, 0:16], -1.0,
                                                   None, OP.mult),
                         waits=[t_s1l], inc="sV")
            t_cb1 = P.op("vector",
                         lambda e: e.tensor_tensor(cbp2[:, :], vsb["cw0"][:, :],
                                                   vsb["cw1"][:, :], OP.add),
                         waits=[t_ld["cw0"], t_ld["cw1"]], inc="sV")
            t_cb2 = P.op("vector",
                         lambda e: e.tensor_tensor(cbp2[:, :], cbp2[:, :],
                                                   vsb["cw2"][:, :], OP.add),
                         waits=[t_cb1, t_ld["cw2"]], inc="sV")
            t_cb3 = P.op("vector",
                         lambda e: e.tensor_tensor(cbp2[:, :], cbp2[:, :],
                                                   vsb["cw3"][:, :], OP.add),
                         waits=[t_cb2, t_ld["cw3"]], inc="sV")
            t_cb4 = P.op("vector",
                         lambda e: e.tensor_tensor(cbp2[:, :], cbp2[:, :],
                                                   s1c[:, 16:24], OP.mult),
                         waits=[t_cb3, t_s1n], inc="sV")
            t_cb5 = P.op("vector",
                         lambda e: e.tensor_tensor(cbp2[:, :], cbp2[:, :],
                                                   vsb["cb"][:, :], OP.add),
                         waits=[t_cb4, t_ld["cb"]], inc="sV")
            t_padf = P.op("vector",
                          lambda e: e.tensor_scalar(
                              pad[:, :, 0:3],
                              s1c[:, 16:24].unsqueeze(2).broadcast_to([128, 8, 3]),
                              -1.0, None, OP.mult),
                          waits=[t_padz, t_s1l], inc="sV")

            t_xcs_free = [[] for _ in range(8)]
            t_szc_free = [None, None]
            t_sils = [None] * 8
            t_stxs = [None] * 8
            t_psA_free = None
            t_psB_free = None
            t_dbc_evac = None
            t_dbc_st = [None, None]
            t_sq_rd = [None, None]
            t_stats_done = None
            t_row_free = []
            t_xch_free = [[], []]
            t_plane_free = [[], []]
            t_tail = [None] * G
            t_convo_free = None
            t_dbc_mm = None
            t_evt_free = [None, None]
            t_xl = [None] * NCH
            t_smm = [None] * NCH
            t_bcast = [None] * NCH

            def p1_load(c):
                xsl = c % 2
                t_xl[c] = P.dma("dX", xch[:, xsl, :, :],
                                xT.rearrange("(g p) t -> p g t", p=128)
                                [:, :, c * CW:(c + 1) * CW],
                                waits=t_xch_free[xsl])

            def p1_stats(c):
                nonlocal t_stats_done, t_row_free
                xsl = c % 2
                cs = slice(c * CW, (c + 1) * CW)
                t_x = t_xl[c]
                t_sq = [None, None]
                t_mmx = None
                t_mmx2 = None
                for g in range(G):
                    sl = g % 2
                    t_sq[sl] = P.op("vector",
                                    lambda e, g=g, sl=sl, xsl=xsl: e.tensor_tensor(
                                        xsq[:, sl, :], xch[:, xsl, g, :],
                                        xch[:, xsl, g, :], OP.mult),
                                    waits=[t_x, t_sq_rd[sl]], inc="sV")
                    t_mmx = P.op("tensor",
                                 lambda e, g=g, xsl=xsl, ps=psSX: e.matmul(
                                     ps[:, :], ones_sb[:, :], xch[:, xsl, g, :],
                                     start=(g == 0), stop=(g == G - 1)),
                                 waits=[t_x, t_stats_done if g == 0 else None],
                                 inc="sT")
                    t_mmx2 = P.op("tensor",
                                  lambda e, g=g, sl=sl, ps=psSX2: e.matmul(
                                      ps[:, :], ones_sb[:, :], xsq[:, sl, :],
                                      start=(g == 0), stop=(g == G - 1)),
                                  waits=[t_sq[sl], t_stats_done if g == 0 else None],
                                  inc="sT")
                    t_sq_rd[sl] = t_mmx2
                t_smm[c] = t_mmx
                t_mu = P.op("vector",
                            lambda e, ps=psSX: e.tensor_scalar(row(0), ps[:, :],
                                                               INV_D, None, OP.mult),
                            waits=[t_mmx] + t_row_free, inc="sV")
                t_e2 = P.op("vector",
                            lambda e, ps=psSX2: e.tensor_scalar(row(1), ps[:, :],
                                                                INV_D, None, OP.mult),
                            waits=[t_mmx2], inc="sV")
                t_m2 = P.op("vector", lambda e: e.tensor_tensor(row(2), row(0), row(0),
                                                                OP.mult),
                            waits=[t_mu], inc="sV")
                t_v1 = P.op("vector", lambda e: e.tensor_tensor(row(3), row(1), row(2),
                                                                OP.subtract),
                            waits=[t_e2, t_m2], inc="sV")
                t_ln = P.op("scalar", lambda e: e.activation(row(2), row(3), AF.Ln,
                                                             bias=EPS_LN),
                            waits=[t_v1], inc="sC")
                t_r1 = P.op("scalar", lambda e: e.activation(row(4), row(2), AF.Exp,
                                                             scale=-0.5),
                            waits=[t_ln], inc="sC")
                t_v2a = P.op("vector", lambda e: e.tensor_tensor(row(2), row(4), row(4),
                                                                 OP.mult),
                             waits=[t_r1], inc="sV")
                t_v2 = P.op("vector", lambda e: e.tensor_tensor(row(2), row(3), row(2),
                                                                OP.mult),
                            waits=[t_v2a], inc="sV")
                t_ln2 = P.op("scalar", lambda e: e.activation(row(2), row(2), AF.Ln,
                                                              bias=EPS_LN),
                             waits=[t_v2], inc="sC")
                t_r2 = P.op("scalar", lambda e: e.activation(row(5), row(2), AF.Exp,
                                                             scale=-0.5),
                             waits=[t_ln2], inc="sC")
                t_R = P.op("vector", lambda e: e.tensor_tensor(row(6), row(4), row(5),
                                                               OP.mult),
                           waits=[t_r2], inc="sV")
                t_RM = P.op("vector", lambda e: e.tensor_tensor(row(2), row(6), row(0),
                                                                OP.mult),
                            waits=[t_R], inc="sV")
                t_stats_done = t_e2
                t_sR = P.dma("dRow", rrow_h[:, cs], row(6), waits=[t_R])
                t_sRM = P.dma("dRow", murow_h[:, cs], row(2), waits=[t_RM])
                t_bR = P.dma("dRow", rbp[:, xsl, :],
                             rrow_h[:, cs].squeeze(0).partition_broadcast(128),
                             waits=[t_sR] + t_plane_free[xsl])
                t_bRM = P.dma("dRow", rmbp[:, xsl, :],
                              murow_h[:, cs].squeeze(0).partition_broadcast(128),
                              waits=[t_sRM])
                t_row_free = [t_sR, t_sRM]
                t_bcast[c] = (t_bR, t_bRM)

            p1_load(0)
            p1_stats(0)
            p1_load(1)
            for c in range(NCH):
                xsl = c % 2
                cs = slice(c * CW, (c + 1) * CW)
                if c + 1 < NCH:
                    p1_stats(c + 1)
                t_x = t_xl[c]
                t_bR, t_bRM = t_bcast[c]
                t_mm_last = None
                for ct in range(16):
                    it = c * 16 + ct
                    if it % 3 == 0:
                        pump_A()
                    elif it % 3 == 2:
                        pump_B()
                    ps = psA if ct % 2 == 0 else psB
                    ps_free = t_psA_free if ct % 2 == 0 else t_psB_free
                    t_mm = None
                    for g in range(G):
                        t_mm = P.op("tensor",
                                    lambda e, g=g, ct=ct, ps=ps, xsl=xsl: e.matmul(
                                        ps[:, :],
                                        winm_sb[:, g, ct * 128:(ct + 1) * 128],
                                        xch[:, xsl, g, :], start=(g == 0),
                                        stop=(g == G - 1)),
                                    waits=[t_x, t_wsc[g],
                                           ps_free if g == 0 else None],
                                    inc="sT")
                    t_mm_last = t_mm
                    esl = ct % 2
                    t_e1 = P.op("vector",
                                lambda e, esl=esl, ps=ps, xsl=xsl: e.tensor_tensor(
                                    evt[:, esl, :], ps[:, :], rbp[:, xsl, :],
                                    OP.mult),
                                waits=[t_mm, t_bR, t_evt_free[esl]], inc="sV")
                    if ct < 8:
                        dst = pad[:, ct, 3:515]
                    else:
                        dst = None
                    if ct < 8:
                        t_e2v = P.op("vector",
                                     lambda e, ct=ct, esl=esl, xsl=xsl:
                                     e.scalar_tensor_tensor(
                                         pad[:, ct, 3:515], rmbp[:, xsl, :],
                                         s1c[:, ct:ct + 1], evt[:, esl, :],
                                         OP.mult, OP.add),
                                     waits=[t_e1, t_bRM, t_s1n, t_padf,
                                            t_tail[ct]], inc="sV")
                        t_evt_free[esl] = t_e2v
                        t_c = P.op("vector",
                                   lambda e, ct=ct: e.tensor_scalar(
                                       convo[:, :], pad[:, ct, 0:CW],
                                       vsb["cw0"][:, ct:ct + 1], cbp2[:, ct:ct + 1],
                                       OP.mult, OP.add),
                                   waits=[t_e2v, t_ld["cw0"], t_cb5,
                                          t_convo_free], inc="sV")
                        for k in (1, 2, 3):
                            t_c = P.op("vector",
                                       lambda e, ct=ct, k=k: e.scalar_tensor_tensor(
                                           convo[:, :], pad[:, ct, k:k + CW],
                                           vsb["cw%d" % k][:, ct:ct + 1], convo[:, :],
                                           OP.mult, OP.add),
                                       waits=[t_c, t_ld["cw%d" % k]], inc="sV")
                        t_tail[ct] = P.op("vector",
                                          lambda e, ct=ct: e.tensor_copy(
                                              pad[:, ct, 0:3], pad[:, ct, 512:515]),
                                          waits=[t_c], inc="sV")
                        t_sil = P.op("scalar",
                                     lambda e, ct=ct: e.activation(xcs[:, ct, :],
                                                                   convo[:, :], AF.Silu),
                                     waits=[t_c] + t_xcs_free[ct], inc="sC")
                        t_convo_free = t_sil
                        t_stx = P.dma("dS1",
                                      xc_h.rearrange("(g p) t -> p g t", p=128)[:, ct, cs],
                                      xcs[:, ct, :], waits=[t_sil])
                        t_sils[ct] = t_sil
                        t_stxs[ct] = t_stx
                        if ct % 2 == 0:
                            t_psA_free = t_e1
                        else:
                            t_psB_free = t_e1
                    else:
                        g2 = ct - 8
                        sl2 = g2 % 2
                        t_e2v = P.op("vector",
                                     lambda e, ct=ct, esl=esl, xsl=xsl:
                                     e.scalar_tensor_tensor(
                                         evt[:, esl, :], rmbp[:, xsl, :],
                                         s1c[:, ct:ct + 1], evt[:, esl, :],
                                         OP.mult, OP.add),
                                     waits=[t_e1, t_bRM, t_s1n], inc="sV")
                        t_sz = P.op("scalar",
                                    lambda e, sl2=sl2, esl=esl, ct=ct: e.activation(
                                        szc[:, sl2, :], evt[:, esl, :], AF.Silu,
                                        bias=s1c[:, 16 + ct:17 + ct]),
                                    waits=[t_e2v, t_s1l, t_szc_free[sl2]], inc="sC")
                        t_evt_free[esl] = t_sz
                        t_szd = P.dma("dS2",
                                      sz_h.rearrange("(g p) t -> p g t", p=128)[:, g2, cs],
                                      szc[:, sl2, :], waits=[t_sz])
                        t_szc_free[sl2] = t_szd
                        if ct % 2 == 0:
                            t_psA_free = t_e1
                        else:
                            t_psB_free = t_e1
                t_plane_free[xsl] = [t_e2v]
                t_xch_free[xsl] = [t_smm[c], t_mm_last]
                if c + 2 < NCH:
                    p1_load(c + 2)
                for ct in range(8):
                    t_xp = P.op("tensor",
                                lambda e, ct=ct: e.matmul(
                                    psDBC[:, :], xpw_sb[:, ct, :], xcs[:, ct, :],
                                    start=(ct == 0), stop=(ct == 7)),
                                waits=[t_sils[ct], t_xpw,
                                       t_dbc_evac if ct == 0 else None],
                                inc="sT")
                    t_xcs_free[ct] = [t_stxs[ct], t_xp]
                dsl = c % 2
                t_dbc_evac = P.op("scalar",
                                  lambda e, dsl=dsl: e.copy(dbcp[:, dsl, :],
                                                            psDBC[:, :]),
                                  waits=[t_xp, t_dbc_st[dsl]], inc="sC")
                t_dbc_st[dsl] = P.dma("gB", dbc_bnc[:, cs], dbcp[:, dsl, :],
                                      waits=[t_dbc_evac])
            while p0_i[0] < len(p0_units):
                pump_A()
                pump_B()
            t_dbcs = t_dbc_st[(NCH - 1) % 2]
            t_ar = P.op("gpsimd",
                        lambda e: e.collective_compute(
                            "AllReduce", OP.add, replica_groups=PAIRS,
                            ins=[dbc_bnc.opt()], outs=[dbc_red.opt()]),
                        waits=[t_dbcs], inc="sCC")
            P.barrier()

        with ExitStack() as p2:
            wotile = SB("wotile", [128, 2, G, 128], stack=p2)
            xcc = SB("xcc", [128, 2, G, CW], stack=p2)
            szz = SB("szz", [128, G, CW], stack=p2)
            dt_sb = SB("dt_sb", [128, G, CW], stack=p2)
            dtx = SB("dtx", [128, G, CW], stack=p2)
            d0p = SB("d0p", [128, 2, CW], stack=p2)
            bbuf = SB("bbuf", [128, 2, CW], stack=p2)
            cbuf = SB("cbuf", [128, 2, CW], stack=p2)
            ubuf = SB("ubuf", [128, 2, G, CW], stack=p2)
            hbuf = SB("hbuf", [128, G, CW], stack=p2)
            tbuf = SB("tbuf", [128, G, CW], stack=p2)
            ybuf = SB("ybuf", [128, G, CW], stack=p2)
            y2c = SB("y2c", [128, G, CW], stack=p2)
            wos = SB("wos", [128, 2, CW], stack=p2)
            psA = PS("psA2", [128, CW], stack=p2)
            psB = PS("psB2", [128, CW], stack=p2)
            psab = [PS("psab0", [128, CW], stack=p2), PS("psab1", [128, CW], stack=p2)]
            pscb = [PS("pscb0", [128, CW], stack=p2), PS("pscb1", [128, CW], stack=p2)]
            psbb = [PS("psbb0", [128, CW], stack=p2), PS("psbb1", [128, CW], stack=p2)]

            def bc(sb, sl, gn):
                return sb[:, sl, :].unsqueeze(1).broadcast_to([128, gn, CW])

            def bcp(ps):
                return ps[:, :].unsqueeze(1).broadcast_to([128, G, CW])

            GH = G // 2
            t_b_free = [None, None]
            t_c_free = [None, None]
            t_carryn = [t_carry0] * NT
            t_d0_free = [None, None]
            t_dt_free = [None]
            t_dtx_free = None
            t_ybuf_free = None
            t_scanned = None
            t_scan_hist = [None, None]
            t_abr = [None, None]
            t_abw = [None, None]
            t_cbr = [None, None]
            t_bbr = [None, None]
            t_xcc_free = [[], []]
            t_szz_free = None
            t_tb_free = None
            t_y2c_free = None
            t_wo_mm = None
            t_wos_free = [None, None]
            wo_pend = []
            t_cm = None
            t_yacc = None
            t_cm_pend = None
            t_wosf = None
            t_wo_free = [None, None]
            t_ps_prev = [None, None]
            t_dtld = {}
            t_xcl = [None] * (NCH + 1)
            t_d0l = [None] * (NCH + 1)

            def emit_wout(cw, t_y2w, m):
                nonlocal t_wosf, t_wo_mm
                hb = cw // 4
                cp = cw % 4
                wsl = m % 2
                t_wl = P.dma("dW", wotile[:, wsl, :, :],
                             woutm.rearrange("(g p) m -> p g m", p=128)
                             [:, :, m * 128:(m + 1) * 128],
                             waits=[t_wo_free[wsl]])
                ps = psA if m % 2 == 0 else psB
                t_mm = None
                for g in range(G):
                    t_mm = P.op("tensor",
                                lambda e, g=g, m=m, ps=ps, wsl=wsl: e.matmul(
                                    ps[:, :], wotile[:, wsl, g, :],
                                    y2c[:, g, :], start=(g == 0),
                                    stop=(g == G - 1)),
                                waits=[t_y2w, t_wl,
                                       t_ps_prev[wsl] if g == 0 else None],
                                inc="sT")
                t_wo_free[wsl] = t_mm
                t_wo_mm = t_mm
                t_ev = P.op("scalar",
                            lambda e, ps=ps, wsl=wsl: e.copy(wos[:, wsl, :],
                                                             ps[:, :]),
                            waits=[t_mm, t_wos_free[wsl]], inc="sC")
                t_ps_prev[wsl] = t_ev
                t_std = P.dma("dOP1" if m % 2 == 0 else "dOP2",
                              opin[cp][hb, m * 128:(m + 1) * 128, :],
                              wos[:, wsl, :], waits=[t_ev])
                t_wos_free[wsl] = t_std
                t_wosf = t_std

            def load_chunk(c):
                if c >= NCH:
                    return
                csl = c % 2
                cs2 = slice(c * CW, (c + 1) * CW)
                t_xcl[c] = P.dma("aXC", xcc[:, csl, :, :],
                                 xc_h.rearrange("(g p) t -> p g t", p=128)[:, :, cs2],
                                 waits=t_xcc_free[csl])
                t_d0l[c] = P.dma("aD0", d0p[:, csl, :],
                                 dbc_red[0:1, cs2].squeeze(0).partition_broadcast(128),
                                 waits=[t_ar, t_d0_free[csl]])

            def build_dt(c, g, extra=()):
                csl = c % 2
                t1 = P.op("scalar",
                          lambda e, g=g, csl=csl, c=c: e.activation(
                              dt_sb[:, g, :], d0p[:, csl, :], AF.Exp,
                              scale=vsb["dtw"][:, g:g + 1],
                              bias=vsb["dtb"][:, g:g + 1]),
                          waits=[t_d0l[c], t_ld["dtw"], t_ld["dtb"],
                                 t_dt_free[0]] + list(extra), inc="sC")
                t2 = P.op("vector",
                          lambda e, g=g, csl=csl: e.tensor_scalar(
                              dt_sb[:, g, :], dt_sb[:, g, :], 1.6e19, None,
                              OP.min),
                          waits=[t1], inc="sV")
                t4 = P.op("scalar",
                          lambda e, g=g, csl=csl: e.activation(
                              dt_sb[:, g, :], dt_sb[:, g, :], AF.Ln,
                              bias=1.0),
                          waits=[t2], inc="sC")
                t_dtld.setdefault(c, [None] * G)[g] = t4
                if g == 0:
                    t_d0_free[csl] = t1

            load_chunk(0)
            load_chunk(1)
            for g in range(G):
                build_dt(0, g)
            for c in range(NCH):
                csl = c % 2
                cs = slice(c * CW, (c + 1) * CW)
                t_sz = P.dma("aSZ", szz[:, :, :],
                             sz_h.rearrange("(g p) t -> p g t", p=128)[:, :, cs],
                             waits=[t_szz_free])
                t_dt = t_dtld[c]
                t_dtx = P.op("vector",
                             lambda e, csl=csl: e.tensor_tensor(
                                 dtx[:, :, :], dt_sb[:, :, :],
                                 xcc[:, csl, :, :], OP.mult),
                             waits=[t_dt[G - 1], t_xcl[c], t_dtx_free], inc="sV")
                for n in range(NT):
                    sl = n % 2
                    if wo_pend and n < G:
                        emit_wout(wo_pend[0][0], wo_pend[0][1], n)
                        if n == G - 1:
                            t_y2c_free = t_wo_mm
                            wo_pend.pop(0)
                    t_bb = P.dma("aB1" if sl == 0 else "aB2", bbuf[:, sl, :],
                                 dbc_red[1 + n:2 + n, cs].squeeze(0)
                                 .partition_broadcast(128),
                                 waits=[t_b_free[sl]])
                    t_cb = P.dma("aC1" if sl == 0 else "aC2", cbuf[:, sl, :],
                                 dbc_red[17 + n:18 + n, cs].squeeze(0)
                                 .partition_broadcast(128),
                                 waits=[t_c_free[sl]])
                    t_cbp = P.op("scalar",
                                 lambda e, sl=sl: e.copy(pscb[sl][:, :],
                                                         cbuf[:, sl, :]),
                                 waits=[t_cb, t_cbr[sl]], inc="sC")
                    t_bbp = P.op("scalar",
                                 lambda e, sl=sl: e.copy(psbb[sl][:, :],
                                                         bbuf[:, sl, :]),
                                 waits=[t_bb, t_bbr[sl]], inc="sC")
                    t_uh = [None, None]
                    t_uh[0] = P.op("gpsimd",
                                   lambda e, sl=sl: e.tensor_tensor(
                                       ubuf[:, sl, 0:GH, :],
                                       dtx[:, 0:GH, :],
                                       bc(bbuf, sl, GH), OP.mult),
                                   waits=[t_dtx, t_bb, t_scan_hist[sl]],
                                   inc="sG")
                    t_uh[1] = P.op("vector",
                                   lambda e, sl=sl: e.tensor_tensor(
                                       ubuf[:, sl, GH:G, :],
                                       dtx[:, GH:G, :],
                                       psbb[sl][:, :].unsqueeze(1)
                                       .broadcast_to([128, GH, CW]), OP.mult),
                                   waits=[t_dtx, t_bbp, t_scan_hist[sl]],
                                   inc="sV")
                    t_bbr[sl] = t_uh[1]
                    if t_cm_pend is not None:
                        t_yacc = P.op("gpsimd",
                                      lambda e: e.tensor_tensor(
                                          ybuf[:, :, :], ybuf[:, :, :], tbuf[:, :, :],
                                          OP.add),
                                      waits=[t_cm_pend, t_yacc], inc="sG")
                        t_tb_free = t_yacc
                        t_cm_pend = None
                    t_ab = [None] * G
                    t_h = None
                    for g in range(G):
                        col = n * G + g
                        t_ab[g] = P.op("scalar",
                                       lambda e, g=g, n=n, csl=csl: e.activation(
                                           psab[g % 2][:, :], dt_sb[:, g, :],
                                           AF.Exp,
                                           scale=asb[:, g * NT + n:g * NT + n + 1]),
                                       waits=[t_dt[g], t_a2, t_abr[g % 2],
                                              t_abw[g % 2]], inc="sC")
                        t_abw[g % 2] = t_ab[g]
                        t_h = P.op("vector",
                                   lambda e, g=g, col=col, sl=sl: e.tensor_tensor_scan(
                                       hbuf[:, g, :], psab[g % 2][:, :],
                                       ubuf[:, sl, g, :],
                                       carry[:, col:col + 1], OP.mult, OP.add),
                                   waits=[t_ab[g], t_uh[g // GH], t_carryn[n], t_cm],
                                   inc="sV")
                        t_abr[g % 2] = t_h
                    t_scanned = t_h
                    t_scan_hist[sl] = t_h
                    if n == NT - 1:
                        t_dt_free[0] = t_ab[G - 1]
                    t_carryn[n] = P.op("scalar",
                                       lambda e, n=n: e.copy(
                                           carry[:, n * G:(n + 1) * G],
                                           hbuf[:, :, CW - 1:CW]
                                           .rearrange("p g one -> p (g one)")),
                                       waits=[t_scanned], inc="sC")
                    if n == 0:
                        t_cm = P.op("vector",
                                    lambda e, sl=sl: e.tensor_tensor(
                                        ybuf[:, :, :], hbuf[:, :, :], bcp(pscb[sl]),
                                        OP.mult),
                                    waits=[t_cbp, t_ybuf_free, t_scanned], inc="sV")
                        t_yacc = t_cm
                        t_cbr[sl] = t_cm
                        t_cm_pend = None
                    else:
                        t_cm = P.op("vector",
                                    lambda e, sl=sl: e.tensor_tensor(
                                        tbuf[:, :, :], hbuf[:, :, :], bcp(pscb[sl]),
                                        OP.mult),
                                    waits=[t_cbp, t_tb_free, t_scanned], inc="sV")
                        t_cbr[sl] = t_cm
                        t_cm_pend = t_cm
                    t_b_free[sl] = t_uh[0]
                    t_c_free[sl] = t_cm
                if t_cm_pend is not None:
                    t_yacc = P.op("gpsimd",
                                  lambda e: e.tensor_tensor(
                                      ybuf[:, :, :], ybuf[:, :, :], tbuf[:, :, :],
                                      OP.add),
                                  waits=[t_cm_pend, t_yacc], inc="sG")
                    t_cm_pend = None
                if c + 1 < NCH:
                    te_ = [None] * G
                    tm_ = [None] * G
                    for g in range(G):
                        te_[g] = P.op("scalar",
                                      lambda e, g=g, c=c: e.activation(
                                          dt_sb[:, g, :], d0p[:, (c + 1) % 2, :],
                                          AF.Exp,
                                          scale=vsb["dtw"][:, g:g + 1],
                                          bias=vsb["dtb"][:, g:g + 1]),
                                      waits=[t_d0l[c + 1], t_ld["dtw"],
                                             t_ld["dtb"], t_dt_free[0]], inc="sC")
                    for g in range(G):
                        tm_[g] = P.op("vector",
                                      lambda e, g=g: e.tensor_scalar(
                                          dt_sb[:, g, :], dt_sb[:, g, :], 1.6e19,
                                          None, OP.min),
                                      waits=[te_[g]], inc="sV")
                    tl_ = [None] * G
                    for g in range(G):
                        tl_[g] = P.op("scalar",
                                      lambda e, g=g: e.activation(
                                          dt_sb[:, g, :], dt_sb[:, g, :], AF.Ln,
                                          bias=1.0),
                                      waits=[tm_[g]], inc="sC")
                    t_dtld[c + 1] = tl_
                    t_d0_free[(c + 1) % 2] = te_[G - 1]
                t_y2s = [None] * G
                for g in range(G):
                    t_y2s[g] = P.op("vector",
                                    lambda e, g=g, csl=csl: e.scalar_tensor_tensor(
                                        y2c[:, g, :], xcc[:, csl, g, :],
                                        vsb["dv"][:, g:g + 1], ybuf[:, g, :],
                                        OP.mult, OP.add),
                                    waits=[t_yacc, t_xcl[c], t_ld["dv"],
                                           t_y2c_free], inc="sV")
                t_y2 = P.op("vector",
                            lambda e: e.tensor_tensor(y2c[:, :, :], y2c[:, :, :],
                                                      szz[:, :, :], OP.mult),
                            waits=[t_y2s[G - 1], t_sz], inc="sV")
                t_ybuf_free = t_y2s[G - 1]
                t_xcc_free[csl] = [t_y2s[G - 1]]
                t_szz_free = t_y2
                load_chunk(c + 2)
                wo_pend.append((c, t_y2))
                if c == NCH - 1:
                    for m in range(G):
                        emit_wout(c, t_y2, m)
                    t_y2c_free = t_wo_mm
                t_dtx_free = t_uh[1]
            t_rs = []
            for cp in range(4):
                t = P.op("gpsimd",
                         lambda e, cp=cp: e.collective_compute(
                             "ReduceScatter", OP.add, replica_groups=PAIRS,
                             ins=[opin[cp].opt()], outs=[opout[cp].opt()]),
                         waits=[("dOP1", P.cnt["dOP1"], "_dma_", -1),
                                ("dOP2", P.cnt["dOP2"], "_dma_", -1)], inc="sCC")
                t_rs.append(t)
            P.barrier()

        with ExitStack() as p3:
            x1 = SB("x1", [128, G, HALF], stack=p3)
            red8 = SB("red8", [128, 12], stack=p3)
            redg = SB("redg", [1, 128], stack=p3)
            sp1 = SB("sp1", [128, FT], stack=p3)
            bq1 = SB("bq1", [128, FT], stack=p3)
            sp2 = SB("sp2", [128, G], stack=p3)
            bq2 = SB("bq2", [128, G], stack=p3)
            xq1 = SB("xq1", [128, G, HALF], BF16, stack=p3)
            w1t_sb = SB("w1t_sb", [128, 2, G, 128], BF16, stack=p3)
            hst = SB("hst", [128, 4, CW], stack=p3)
            p3ps = p3.enter_context(ExitStack())
            psA = PS("psA3", [128, CW], stack=p3ps)
            psB = PS("psB3", [128, CW], stack=p3ps)
            psC = PS("psC3", [128, CW], stack=p3ps)
            psD = PS("psD3", [128, CW], stack=p3ps)
            psF1 = [psA, psB, psC, psD]
            p3a = p3.enter_context(ExitStack())
            psSX = PS("psSX3", [1, CW], stack=p3a)
            psSX2 = PS("psSX23", [1, CW], stack=p3a)
            mu2b = SB("mu2b", [128, HALF], stack=p3a)
            rs2b = SB("rs2b", [128, HALF], stack=p3a)
            tmp1 = SB("tmp1", [128, HALF], stack=p3a)
            mu1bh = SB("mu1bh", [128, HALF], stack=p3a)
            rs1bh = SB("rs1bh", [128, HALF], stack=p3a)
            tmp2 = SB("tmp2", [128, HALF], stack=p3a)
            sq_sb = SB("sq_sb", [128, 2, CW], stack=p3a)
            strow = SB("strow", [1, 7 * CW], stack=p3a)

            def r3(i):
                return strow[:, i * CW:(i + 1) * CW]

            def ffn_stats(src_getter, t_src, n_half, mu_plane, rs_plane,
                          t_prev=None, t_rowf=()):
                t_pst = t_prev
                t_sqr = [None, None]
                toks = []
                for hc in range(n_half // CW):
                    hs = slice(hc * CW, (hc + 1) * CW)
                    t_sq = [None, None]
                    t_mmx = None
                    t_mmx2 = None
                    for g in range(G):
                        sl = g % 2
                        t_sq[sl] = P.op("vector",
                                        lambda e, g=g, sl=sl, hs=hs: e.tensor_tensor(
                                            sq_sb[:, sl, :], src_getter(g)[:, hs],
                                            src_getter(g)[:, hs], OP.mult),
                                        waits=[t_src[g], t_sqr[sl]], inc="sV")
                        t_mmx = P.op("tensor",
                                     lambda e, g=g, hs=hs: e.matmul(
                                         psSX[:, :], ones_sb[:, :],
                                         src_getter(g)[:, hs], start=(g == 0),
                                         stop=(g == G - 1)),
                                     waits=[t_src[g], t_pst if g == 0 else None],
                                     inc="sT")
                        t_mmx2 = P.op("tensor",
                                      lambda e, g=g, sl=sl: e.matmul(
                                          psSX2[:, :], ones_sb[:, :], sq_sb[:, sl, :],
                                          start=(g == 0), stop=(g == G - 1)),
                                      waits=[t_sq[sl], t_pst if g == 0 else None],
                                      inc="sT")
                        t_sqr[sl] = t_mmx2
                    t_mu = P.op("vector",
                                lambda e: e.tensor_scalar(r3(0), psSX[:, :], INV_D,
                                                          None, OP.mult),
                                waits=[t_mmx] + list(t_rowf), inc="sV")
                    t_e2 = P.op("vector",
                                lambda e: e.tensor_scalar(r3(1), psSX2[:, :], INV_D,
                                                          None, OP.mult),
                                waits=[t_mmx2], inc="sV")
                    t_m2 = P.op("vector",
                                lambda e: e.tensor_tensor(r3(2), r3(0), r3(0),
                                                          OP.mult),
                                waits=[t_mu], inc="sV")
                    t_v1 = P.op("vector",
                                lambda e: e.tensor_tensor(r3(3), r3(1), r3(2),
                                                          OP.subtract),
                                waits=[t_e2, t_m2], inc="sV")
                    t_ln = P.op("scalar",
                                lambda e: e.activation(r3(2), r3(3), AF.Ln,
                                                       bias=EPS_LN),
                                waits=[t_v1], inc="sC")
                    t_rs1 = P.op("scalar",
                                 lambda e: e.activation(r3(4), r3(2), AF.Exp,
                                                        scale=-0.5),
                                 waits=[t_ln], inc="sC")
                    t_pst = t_e2
                    t_smu = P.dma("dRow", mu2row_h[:, hs], r3(0),
                                  waits=[t_m2] + list(t_rowf))
                    t_srs = P.dma("dRow", rst2row_h[:, hs], r3(4), waits=[t_rs1])
                    t_rowf = (t_smu, t_srs)
                    t_bmu = P.dma("dRow", mu_plane[:, hs],
                                  mu2row_h[:, hs].squeeze(0).partition_broadcast(128),
                                  waits=[t_smu])
                    t_brs = P.dma("dRow", rs_plane[:, hs],
                                  rst2row_h[:, hs].squeeze(0).partition_broadcast(128),
                                  waits=[t_srs])
                    toks.append((t_bmu, t_brs))
                return toks[-1][0], toks[-1][1], t_mmx2, t_rowf

            t_xm0 = P.dma("dX", x1[:, :, :],
                          xmine.rearrange("(g p) t -> p g t", p=128))
            t_xm = [t_xm0] * G
            t_pl = ffn_stats(lambda g: x1[:, g, :], t_xm, HALF, mu1bh, rs1bh)
            t_st1_te = t_pl[2]
            t_x1 = [None] * G
            t_tmp1_free = None
            t_tmp2_free = None
            for g in range(G):
                t1 = P.op("vector",
                          lambda e, g=g: e.tensor_tensor(tmp1[:, :], x1[:, g, :],
                                                         mu1bh[:, :], OP.subtract),
                          waits=[t_xm0, t_pl[0], t_tmp1_free], inc="sV")
                t2 = P.op("vector",
                          lambda e, g=g: e.tensor_tensor(tmp1[:, :], tmp1[:, :],
                                                         rs1bh[:, :], OP.mult),
                          waits=[t1, t_pl[1]], inc="sV")
                t3 = P.op("vector",
                          lambda e, g=g: e.tensor_tensor(x1[:, g, :], x1[:, g, :],
                                                         tmp1[:, :], OP.add),
                          waits=[t2, t_st1_te], inc="sV")
                t_tmp1_free = t3
                t_op = []
                for cp in range(4):
                    t_op.append(P.dma("aOP1" if cp % 2 == 0 else "aOP2", tmp2[:, cp * CW:(cp + 1) * CW],
                                      opout[cp][g * 128:(g + 1) * 128, :],
                                      waits=[t_rs[cp], t_tmp2_free]))
                t_x1[g] = P.op("vector",
                               lambda e, g=g: e.tensor_tensor(x1[:, g, :], x1[:, g, :],
                                                              tmp2[:, :], OP.add),
                               waits=[t3] + t_op, inc="sV")
                t_tmp2_free = t_x1[g]

            t_pl2 = ffn_stats(lambda g: x1[:, g, :], t_x1, HALF, mu2b, rs2b,
                              t_prev=t_pl[2], t_rowf=t_pl[3])

            t_red = None
            for g in range(G):
                t1 = P.op("vector",
                          lambda e, g=g: e.tensor_tensor(tmp1[:, :], x1[:, g, :],
                                                         mu2b[:, :], OP.subtract),
                          waits=[t_x1[g], t_pl2[0], t_tmp1_free], inc="sV")
                t2 = P.op("vector",
                          lambda e, g=g: e.tensor_tensor(tmp1[:, :], tmp1[:, :],
                                                         rs2b[:, :], OP.mult),
                          waits=[t1, t_pl2[1]], inc="sV")
                t_red = P.op("vector",
                             lambda e, g=g: e.tensor_reduce(red8[:, g:g + 1],
                                                            tmp1[:, :], AXX, OP.max),
                             waits=[t2], inc="sV")
                t_tmp1_free = t_red
            t_red = P.op("vector",
                         lambda e: e.tensor_reduce(red8[:, 8:9], red8[:, 0:8], AXX,
                                                   OP.max),
                         waits=[t_red], inc="sV")
            t_gv = P.dma("dMx", vbnc, red8[:, 8:9], waits=[t_red])
            t_gl = P.dma("dMx", redg[:, :], vbnc.rearrange("p one -> one p"),
                         waits=[t_gv])
            t_mx = P.op("vector",
                        lambda e: e.tensor_reduce(srow[:, 0:1], redg[:, :], AXX, OP.max),
                        waits=[t_gl], inc="sV")
            t_mxe = P.op("vector",
                         lambda e: e.tensor_scalar(srow[:, 0:1], srow[:, 0:1], EPS_Q,
                                                   None, OP.max),
                         waits=[t_mx], inc="sV")
            t_mxd = P.dma("gB", mx_in, srow[:, 0:1], waits=[t_mxe])
            t_armx = P.op("gpsimd",
                          lambda e: e.collective_compute(
                              "AllReduce", OP.max, replica_groups=ALL8,
                              ins=[mx_in.opt()], outs=[mx_out.opt()]),
                          waits=[t_mxd], inc="sCC")
            t_m1b = P.dma("dMx", scal[:, 0:1],
                          mx_out.squeeze(0).partition_broadcast(128), waits=[t_armx])
            t_s1 = P.op("vector",
                        lambda e: e.tensor_scalar(scal[:, 1:2], scal[:, 0:1],
                                                  1.0 / 255.0, None, OP.mult),
                        waits=[t_m1b], inc="sV")
            t_r1 = P.op("vector", lambda e: e.reciprocal(scal[:, 2:3], scal[:, 1:2]),
                        waits=[t_s1], inc="sV")
            t_sp1 = P.op("vector",
                         lambda e: e.tensor_scalar(sp1[:, :], sw1[:, :], scal[:, 1:2],
                                                   None, OP.mult),
                         waits=[t_s1], inc="sV")
            t_rq = P.op("vector", lambda e: e.reciprocal(bq1[:, :], sp1[:, :]),
                        waits=[t_sp1], inc="sV")
            t_b1 = P.op("vector",
                        lambda e: e.tensor_tensor(bq1[:, :], vsb["b1v"][:, :], bq1[:, :],
                                                  OP.mult),
                        waits=[t_rq, t_ld["b1v"]], inc="sV")
            t_b1r = P.op("vector",
                         lambda e: e.tensor_scalar(bq1[:, :], bq1[:, :], MAGIC, MAGIC,
                                                   OP.add, OP.subtract),
                         waits=[t_b1], inc="sV")
            t_bq1 = P.op("vector",
                         lambda e: e.tensor_tensor(bq1[:, :], bq1[:, :], sp1[:, :],
                                                   OP.mult),
                         waits=[t_b1r], inc="sV")

            t_xq1 = [None] * G
            for g in range(G):
                t1 = P.op("vector",
                          lambda e, g=g: e.tensor_tensor(tmp1[:, :], x1[:, g, :],
                                                         mu2b[:, :], OP.subtract),
                          waits=[t_x1[g], t_tmp1_free], inc="sV")
                t2 = P.op("vector",
                          lambda e, g=g: e.tensor_tensor(tmp1[:, :], tmp1[:, :],
                                                         rs2b[:, :], OP.mult),
                          waits=[t1], inc="sV")
                t3 = P.op("vector",
                          lambda e, g=g: e.tensor_scalar(tmp1[:, :], tmp1[:, :],
                                                         scal[:, 2:3], MAGIC,
                                                         OP.mult, OP.add),
                          waits=[t2, t_r1], inc="sV")
                t_xq1[g] = P.op("vector",
                                lambda e, g=g: e.tensor_scalar(xq1[:, g, :], tmp1[:, :],
                                                               MAGIC, 0.0,
                                                               OP.subtract, OP.max),
                                waits=[t3], inc="sV")
                t_tmp1_free = t_xq1[g]

            P.barrier()
            p3a.close()
            wq2a = SB("wq2a", [128, FT, DM], BF16, stack=p3)
            t_w2a = [None] * 4
            for qc in range(4):
                t_w2a[qc] = P.dma("aW2" if qc % 2 == 0 else "aW",
                                  wq2a[:, qc * G:(qc + 1) * G, :],
                                  w2q_h.rearrange("(f p) d -> p f d", p=128)
                                  [:, qc * G:(qc + 1) * G, :])

            t_w1_free = [None, None]
            t_ps_free = [None, None, None, None]
            t_hst_free = [[], [], [], []]
            t_hmax = P.op("vector", lambda e: e.memset(red8[:, 9:10], 0.0), inc="sV")
            t_hd = None
            for ft in range(FT):
                sl = ft % 2
                t_w = P.dma("aW1", w1t_sb[:, sl, :, :],
                            w1q_h.rearrange("(g p) f -> p g f", p=128)
                            [:, :, ft * 128:(ft + 1) * 128],
                            waits=[t_w1_free[sl]])
                t_mm_last = None
                for hc in range(4):
                    hs = slice(hc * CW, (hc + 1) * CW)
                    psl = (ft * 4 + hc) % 4
                    ps = psF1[psl]
                    t_mm = None
                    for g in range(G):
                        t_mm = P.op("tensor",
                                    lambda e, g=g, sl=sl, hs=hs, ps=ps: e.matmul(
                                        ps[:, :], w1t_sb[:, sl, g, :], xq1[:, g, hs],
                                        start=(g == 0), stop=(g == G - 1)),
                                    waits=[t_w, t_xq1[g],
                                           t_ps_free[psl] if g == 0 else None],
                                    inc="sT")
                    t_hv = P.op("scalar",
                                lambda e, ft=ft, psl=psl, ps=ps: e.activation(
                                    hst[:, psl, :], ps[:, :], AF.Relu,
                                    scale=sp1[:, ft:ft + 1], bias=bq1[:, ft:ft + 1]),
                                waits=[t_mm, t_sp1, t_bq1]
                                      + (t_hst_free[psl] or []), inc="sC")
                    t_ps_free[psl] = t_hv
                    t_hd = P.dma("dH%d" % ((ft * 4 + hc) % 4 + 1),
                                 h_h.rearrange("(f p) t -> p f t", p=128)[:, ft, hs],
                                 hst[:, psl, :], waits=[t_hv])
                    t_hm = P.op("vector",
                                lambda e, psl=psl: e.tensor_reduce(
                                    red8[:, 10:11], hst[:, psl, :], AXX, OP.max),
                                waits=[t_hv, t_hmax], inc="sV")
                    t_hmax = P.op("vector",
                                  lambda e: e.tensor_tensor(red8[:, 9:10], red8[:, 9:10],
                                                            red8[:, 10:11], OP.max),
                                  waits=[t_hm], inc="sV")
                    t_hst_free[psl] = [t_hd, t_hm]
                    t_mm_last = t_mm
                t_w1_free[sl] = t_mm_last
            t_gv2 = P.dma("dMx", vbnc, red8[:, 9:10], waits=[t_hmax])
            t_gl2 = P.dma("dMx", redg[:, :], vbnc.rearrange("p one -> one p"),
                          waits=[t_gv2])
            t_mx2 = P.op("vector",
                         lambda e: e.tensor_reduce(srow[:, 1:2], redg[:, :], AXX,
                                                   OP.max),
                         waits=[t_gl2], inc="sV")
            t_mxe2 = P.op("vector",
                          lambda e: e.tensor_scalar(srow[:, 1:2], srow[:, 1:2], EPS_Q,
                                                    None, OP.max),
                          waits=[t_mx2], inc="sV")
            t_mxd2 = P.dma("gB", mx2_in, srow[:, 1:2], waits=[t_mxe2])
            t_armx2 = P.op("gpsimd",
                           lambda e: e.collective_compute(
                               "AllReduce", OP.max, replica_groups=ALL8,
                               ins=[mx2_in.opt()], outs=[mx2_out.opt()]),
                           waits=[t_mxd2], inc="sCC")
            t_m2b = P.dma("dMx", scal[:, 3:4],
                          mx2_out.squeeze(0).partition_broadcast(128), waits=[t_armx2])
            t_s2 = P.op("vector",
                        lambda e: e.tensor_scalar(scal[:, 4:5], scal[:, 3:4],
                                                  1.0 / 255.0, None, OP.mult),
                        waits=[t_m2b], inc="sV")
            t_r2 = P.op("vector", lambda e: e.reciprocal(scal[:, 5:6], scal[:, 4:5]),
                        waits=[t_s2], inc="sV")
            t_sp2 = P.op("vector",
                         lambda e: e.tensor_scalar(sp2[:, :], sw2[:, :], scal[:, 4:5],
                                                   None, OP.mult),
                         waits=[t_s2], inc="sV")
            t_rq2 = P.op("vector", lambda e: e.reciprocal(bq2[:, :], sp2[:, :]),
                         waits=[t_sp2], inc="sV")
            t_b2a = P.op("vector",
                         lambda e: e.tensor_tensor(bq2[:, :], vsb["b2v"][:, :],
                                                   bq2[:, :], OP.mult),
                         waits=[t_rq2, t_ld["b2v"]], inc="sV")
            t_b2b = P.op("vector",
                         lambda e: e.tensor_scalar(bq2[:, :], bq2[:, :], MAGIC, MAGIC,
                                                   OP.add, OP.subtract),
                         waits=[t_b2a], inc="sV")
            t_bq2 = P.op("vector",
                         lambda e: e.tensor_tensor(bq2[:, :], bq2[:, :], sp2[:, :],
                                                   OP.mult),
                         waits=[t_b2b], inc="sV")

            P.barrier()
            p3ps.close()
            psm = [PS("psm%d" % i, [128, CW], stack=p3) for i in range(G)]
            xq2 = SB("xq2", [128, FT, CW], BF16, stack=p3)
            hld = hst
            ost = hst[:, 2:4, :]
            t_hall = [("dH%d" % i, P.cnt["dH%d" % i], "_dma_", -1)
                      for i in (1, 2, 3, 4)]
            t_out_last = None
            t_hl_free = [None, None]
            t_xq2_free = [None] * FT
            t_ost_free = [None, None]
            t_psm_free = [None] * G
            for hc in range(4):
                hs = slice(hc * CW, (hc + 1) * CW)
                t_xq2t = [None] * FT
                for ft in range(FT):
                    sl = ft % 2
                    t_hl = P.dma("aH%d" % (ft % 4 + 1), hld[:, sl, :],
                                 h_h.rearrange("(f p) t -> p f t", p=128)[:, ft, hs],
                                 waits=t_hall + [t_hl_free[sl]])
                    t1 = P.op("vector",
                              lambda e, sl=sl: e.tensor_scalar(
                                  hld[:, sl, :], hld[:, sl, :], scal[:, 5:6], MAGIC,
                                  OP.mult, OP.add),
                              waits=[t_hl, t_r2], inc="sV")
                    t_xq2t[ft] = P.op("vector",
                                      lambda e, ft=ft, sl=sl: e.tensor_scalar(
                                          xq2[:, ft, :], hld[:, sl, :], MAGIC, None,
                                          OP.subtract),
                                      waits=[t1, t_xq2_free[ft]], inc="sV")
                    t_hl_free[sl] = t_xq2t[ft]
                t_mm = None
                t_ftg_done = [None] * 4
                for ftg in range(4):
                    for m in range(G):
                        for fl in range(G):
                            ft = ftg * G + fl
                            t_mm = P.op("tensor",
                                        lambda e, ft=ft, m=m,
                                        ps=psm[m]: e.matmul(
                                            ps[:, :],
                                            wq2a[:, ft, m * 128:(m + 1) * 128],
                                            xq2[:, ft, :],
                                            start=(ft == 0),
                                            stop=(ft == FT - 1)),
                                        waits=[t_xq2t[ft], t_w2a[ftg],
                                               t_psm_free[m] if ft == 0
                                               else None],
                                        inc="sT")
                    t_ftg_done[ftg] = t_mm
                for m in range(G):
                    psl = m % 2
                    t_o1 = P.op("vector",
                                lambda e, m=m, psl=psl, ps=psm[m]: e.tensor_scalar(
                                    ost[:, psl, :], ps[:, :], sp2[:, m:m + 1],
                                    bq2[:, m:m + 1], OP.mult, OP.add),
                                waits=[t_mm, t_sp2, t_bq2, t_ost_free[psl]], inc="sV")
                    t_psm_free[m] = t_o1
                    t_o2 = P.op("vector",
                                lambda e, m=m, psl=psl, hs=hs: e.tensor_tensor(
                                    ost[:, psl, :], ost[:, psl, :], x1[:, m, hs],
                                    OP.add),
                                waits=[t_o1], inc="sV")
                    t_out_last = P.dma("dOut1" if m % 2 == 0 else "dOut2",
                                       out_h[m * 128:(m + 1) * 128, hs],
                                       ost[:, psl, :], waits=[t_o2])
                    t_ost_free[psl] = t_out_last
                for ft in range(FT):
                    t_xq2_free[ft] = t_ftg_done[ft // G]
            P.op("sync", None, waits=[t_out_last])

        block = es.enter_context(nc.Block())
        P.replay(block)
    return nc


_NC_CACHE = None


def kernel(L0_attn_x, L0_attn_norm_w, L0_attn_norm_b, L0_attn_W_in, L0_attn_conv_w,
           L0_attn_conv_b, L0_attn_x_proj_w, L0_attn_dt_w, L0_attn_dt_b, L0_attn_A_log,
           L0_attn_D, L0_attn_W_out, L0_ffn_w1, L0_ffn_b1, L0_ffn_w2, L0_ffn_b2):
    global _NC_CACHE
    f32 = np.float32
    x = np.asarray(L0_attn_x, f32)
    W_in = np.asarray(L0_attn_W_in, f32)
    conv_w = np.asarray(L0_attn_conv_w, f32)
    ident = np.eye(128).astype(np.float32)
    import ml_dtypes
    ident_bf = ident.astype(ml_dtypes.bfloat16)

    def v2d(v, g):
        return np.ascontiguousarray(np.asarray(v, f32).reshape(g, 128).T)

    in_maps = []
    for k in range(8):
        b, j = k // 2, k % 2
        ch = slice(j * 1024, (j + 1) * 1024)
        m = {
            "xT": np.ascontiguousarray(x[b].T),
            "xmine": np.ascontiguousarray(x[b][j * HALF:(j + 1) * HALF].T),
            "winm": np.ascontiguousarray(
                np.concatenate([W_in[:, ch],
                                W_in[:, 2048 + j * 1024:2048 + (j + 1) * 1024]],
                               axis=1)),
            "xpw": np.ascontiguousarray(np.asarray(L0_attn_x_proj_w, f32)[ch]),
            "woutm": np.ascontiguousarray(np.asarray(L0_attn_W_out, f32)[ch]),
            "w1n": np.ascontiguousarray(np.asarray(L0_ffn_w1, f32)),
            "w2n": np.ascontiguousarray(np.asarray(L0_ffn_w2, f32)),
            "alog2d": np.ascontiguousarray(
                np.asarray(L0_attn_A_log, f32)[ch].reshape(8, 128, 16)
                .transpose(1, 0, 2).reshape(128, 128)),
            "ident": ident_bf,
            "cw0": v2d(conv_w[ch, 0], 8),
            "cw1": v2d(conv_w[ch, 1], 8),
            "cw2": v2d(conv_w[ch, 2], 8),
            "cw3": v2d(conv_w[ch, 3], 8),
            "cb": v2d(np.asarray(L0_attn_conv_b, f32)[ch], 8),
            "dv": v2d(np.asarray(L0_attn_D, f32)[ch], 8),
            "dtw": v2d(np.asarray(L0_attn_dt_w, f32)[ch], 8),
            "dtb": v2d(np.asarray(L0_attn_dt_b, f32)[ch], 8),
            "nw": v2d(np.asarray(L0_attn_norm_w, f32), 8),
            "nb": v2d(np.asarray(L0_attn_norm_b, f32), 8),
            "b1v": v2d(np.asarray(L0_ffn_b1, f32), 32),
            "b2v": v2d(np.asarray(L0_ffn_b2, f32), 8),
        }
        in_maps.append(m)

    if _NC_CACHE is None:
        _NC_CACHE = build_nc()
    res = run_bass_kernel_spmd(_NC_CACHE, in_maps, core_ids=list(range(8)))

    y = np.empty((4, L, DM), f32)
    for k in range(8):
        b, j = k // 2, k % 2
        y[b, j * HALF:(j + 1) * HALF, :] = np.asarray(res.results[k]["out"]).T
    return y
